# version 11
# baseline (speedup 1.0000x reference)
import sys
if "/opt/trn_rl_repo" not in sys.path:
    sys.path.insert(0, "/opt/trn_rl_repo")

from contextlib import ExitStack

import numpy as np
import ml_dtypes

import concourse.bass as bass
import concourse.mybir as mybir
from concourse.bass_utils import run_bass_kernel_spmd

F32 = mybir.dt.float32
BF16 = mybir.dt.bfloat16
AX = mybir.AxisListType.X
AF = mybir.ActivationFunctionType
ALU = mybir.AluOpType

ENGS = ("tensor", "vector", "scalar", "sync", "gpsimd")
ESEM = {"tensor": "T", "vector": "V", "scalar": "S", "sync": "Y", "gpsimd": "P"}

D = 1024
NCH = 8
HD = 64
NG = 8
DFF = 4096
NEXP = 8
NPHASE = 4
JCH = 8
NL = 2
EPS = 1e-5
BIGNEG = -1.0e30
SCALE = 0.125


class _Op:
    __slots__ = ("deps", "build", "sem", "amt", "used", "val", "eng", "seq", "cond")

    def __init__(self, eng, build, deps, sem, amt, seq, cond):
        self.eng = eng
        self.build = build
        self.deps = deps
        self.sem = sem
        self.amt = amt
        self.used = False
        self.val = None
        self.seq = seq
        self.cond = cond


class Tok:
    __slots__ = ("op",)

    def __init__(self, op):
        self.op = op


class Prog:
    def __init__(self):
        self.stream = {e: [] for e in ENGS}
        self.n_ops = 0
        self.const_tok = None
        self.cond_ctx = None

    def op(self, eng, build, deps=(), sem=None, amt=1):
        dd = []
        if eng == "scalar" and self.const_tok is not None:
            dd.append(self.const_tok)
        for d in deps:
            if d is None:
                continue
            if isinstance(d, (list, tuple)):
                for x in d:
                    if x is not None:
                        dd.append(x)
            else:
                dd.append(d)
        if eng == "tensor":
            dd = [d for d in dd if d.op.sem != "T"]
        o = _Op(eng, build, dd, sem or ESEM[eng], amt, self.n_ops, self.cond_ctx)
        self.stream[eng].append(o)
        self.n_ops += 1
        return Tok(o)

    def finalize_and_emit(self, block, sems):
        last_by = {}
        for e in ENGS:
            for o in self.stream[e]:
                if o.cond is not None:
                    last_by[(id(o.cond), o.sem)] = o
        for e in ENGS:
            for o in self.stream[e]:
                for idx, d in enumerate(o.deps):
                    dop = d.op
                    if dop.cond is not None and dop.cond is not o.cond:
                        o.deps[idx] = Tok(last_by[(id(dop.cond), dop.sem)])

        consumers = {}
        for e in ENGS:
            for o in self.stream[e]:
                for d in o.deps:
                    consumers.setdefault(id(d.op), []).append(o)
        by_sem = {}
        for e in ENGS:
            for o in self.stream[e]:
                if o.amt == 16:
                    by_sem.setdefault(o.sem, []).append(o)
        for s, dmas in by_sem.items():
            engs = {o.eng for o in dmas}
            assert len(engs) == 1, f"DMA sem {s} used from {engs}"
            for k in range(len(dmas) - 1):
                nxt = dmas[k + 1]
                ok_cond = (None, nxt.cond if nxt.cond is dmas[k].cond else None)
                cs = [c for c in consumers.get(id(dmas[k]), [])
                      if c.seq < nxt.seq and (c.cond is None or c.cond in ok_cond
                                              and c.cond is dmas[k].cond)]
                if cs:
                    nxt.deps.append(Tok(min(cs, key=lambda o: o.seq)))
                else:
                    nxt.deps.append(Tok(dmas[k]))

        for e in ENGS:
            for o in self.stream[e]:
                if o.amt == 16:
                    o.used = True
                if o.cond is not None:
                    o.cond[2].op.used = True
                for d in o.deps:
                    d.op.used = True
        cnt = {}
        for e in ENGS:
            for o in self.stream[e]:
                if o.used:
                    cnt[o.sem] = cnt.get(o.sem, 0) + o.amt
                    o.val = cnt[o.sem]
        self.sem_totals = cnt

        final_waits = {e: [] for e in ENGS}
        for s, dmas in by_sem.items():
            final_waits[dmas[0].eng].append((s, cnt[s]))

        for e in ENGS:
            ops = self.stream[e]
            if not ops and not final_waits[e]:
                continue

            def body(eng, ops=ops, e=e):
                cov = {}

                def emit_one(o):
                    ws = {}
                    for d in o.deps:
                        dop = d.op
                        if dop.val is not None and ws.get(dop.sem, 0) < dop.val:
                            ws[dop.sem] = dop.val
                    for s in sorted(ws):
                        v = ws[s]
                        if cov.get(s, 0) >= v:
                            continue
                        cov[s] = v
                        eng.wait_ge(sems[s], v)
                    inst = o.build(eng)
                    if o.used:
                        inst.then_inc(sems[o.sem], o.amt)

                creg = [None]
                i = 0
                n = len(ops)
                while i < n:
                    o = ops[i]
                    if o.cond is None:
                        emit_one(o)
                        i += 1
                        continue
                    j = i
                    while j < n and ops[j].cond is o.cond:
                        j += 1
                    grp = ops[i:j]
                    _, flag_ap, flag_tok = o.cond
                    ft = flag_tok.op
                    if ft.val is not None and cov.get(ft.sem, 0) < ft.val:
                        cov[ft.sem] = ft.val
                        eng.wait_ge(sems[ft.sem], ft.val)
                    if creg[0] is None:
                        creg[0] = eng.alloc_register(name="condflag_" + e)
                    eng.reg_load(creg[0], flag_ap)
                    deltas = {}
                    for oo in grp:
                        if oo.val is not None:
                            deltas[oo.sem] = deltas.get(oo.sem, 0) + oo.amt
                    outer_cov = dict(cov)
                    with eng.If_ne(creg[0], 0):
                        for oo in grp:
                            emit_one(oo)
                    with eng.Else():
                        for s, v in deltas.items():
                            end_v = max(oo.val for oo in grp
                                        if oo.val is not None and oo.sem == s)
                            eng.wait_ge(sems[s], end_v - v)
                            eng.sem_inc(sems[s], v)
                    eng.end_ifs()
                    cov.clear()
                    cov.update(outer_cov)
                    i = j
                for s, v in final_waits[e]:
                    if cov.get(s, 0) < v:
                        eng.wait_ge(sems[s], v)

            getattr(block, e)(body)


class Slot:

    def __init__(self):
        self.w = []
        self.rs = []

    def for_write(self):
        return list(self.w) + list(self.rs)

    def wrote(self, tok):
        self.w = [tok]
        self.rs = []

    def wrote_multi(self, toks):
        self.w = [t for t in toks if t is not None]
        self.rs = []

    def for_read(self):
        return list(self.w)

    def read(self, tok):
        self.rs.append(tok)


def build_program(t_tiles=16, n_seq=2, debug_stop=None):
    T = t_tiles * 128
    TT = t_tiles
    assert T % 512 == 0
    NGRP = T // 512

    nc = bass.Bass("TRN2", target_bir_lowering=False)

    d_x = nc.dram_tensor("x_in", [n_seq * T, D], F32, kind="ExternalInput")
    d_wqkv = nc.dram_tensor("wqkv", [D, 3 * D], F32, kind="ExternalInput")
    d_wout = nc.dram_tensor("wout", [D, D], F32, kind="ExternalInput")
    d_bout = nc.dram_tensor("bout", [1, D], F32, kind="ExternalInput")
    d_wq = nc.dram_tensor("wq", [NL * D, D], F32, kind="ExternalInput")
    d_wk = nc.dram_tensor("wk", [NL * D, D], F32, kind="ExternalInput")
    d_wv = nc.dram_tensor("wv", [NL * D, D], F32, kind="ExternalInput")
    d_wo = nc.dram_tensor("wo", [NL * D, D], F32, kind="ExternalInput")
    d_lnstack = nc.dram_tensor("lnstack", [NL * 32, 128], F32, kind="ExternalInput")
    d_bqk = nc.dram_tensor("bqk", [NL * 16, 128], F32, kind="ExternalInput")
    d_bv = nc.dram_tensor("bv", [NL, D], F32, kind="ExternalInput")
    d_bo = nc.dram_tensor("bo", [NL, D], F32, kind="ExternalInput")
    d_gw = nc.dram_tensor("gatew", [NL * D, NEXP], F32, kind="ExternalInput")
    d_gb = nc.dram_tensor("gateb", [1, NL * NEXP], F32, kind="ExternalInput")
    d_ew1 = nc.dram_tensor("ew1", [NL * NEXP * D, DFF], F32, kind="ExternalInput")
    d_eb1 = nc.dram_tensor("eb1", [NL * NEXP * 32, 128], F32, kind="ExternalInput")
    d_ew2 = nc.dram_tensor("ew2", [NL * NEXP * DFF, D], F32, kind="ExternalInput")
    d_eb2 = nc.dram_tensor("eb2", [NL * NEXP, D], F32, kind="ExternalInput")
    d_eyeb = nc.dram_tensor("eyeb", [128, 128], BF16, kind="ExternalInput")
    d_eye32 = nc.dram_tensor("eye32", [128, 128], F32, kind="ExternalInput")
    d_onesb = nc.dram_tensor("onesb", [1, 128], BF16, kind="ExternalInput")
    d_ones32 = nc.dram_tensor("ones32", [1, 128], F32, kind="ExternalInput")
    d_mask = nc.dram_tensor("mask", [128, 128], F32, kind="ExternalInput")
    d_out = nc.dram_tensor("out", [n_seq * T, D], F32, kind="ExternalOutput")
    d_xsave = nc.dram_tensor("xsave", [T, D], F32)

    es = ExitStack()
    sbt = lambda name, shape, dt: es.enter_context(nc.sbuf_tensor(name, shape, dt))
    pst_ = lambda name, shape, dt: es.enter_context(nc.psum_tensor(name, shape, dt))

    A = sbt("A", [128, TT * 1024], F32)
    HF = sbt("HF", [128, NCH * T], BF16)
    CB = sbt("CB", [128, NCH * T], BF16)
    QB = sbt("QB", [128, T], BF16)
    KBf = sbt("KB", [128, T], BF16)
    VB = sbt("VB", [128, T], BF16)
    PB = sbt("PB", [128, T], BF16)
    PTB = sbt("PTB", [128, T], BF16)
    DUM = sbt("DUM", [128, 1024], BF16)
    NT = sbt("NT", [128, 2048], BF16)
    NT32 = sbt("NT32", [128, 2048], F32)
    H32 = sbt("H32", [128, 256], F32)
    WS = sbt("WS", [128, 2 * 1024], F32)
    WB = sbt("WB", [128, 3 * 1024], BF16)
    WB2 = sbt("WB2", [128, 8 * 1024], BF16)
    ST = sbt("STAT", [128, 64], F32)
    GS = sbt("GS", [128, 80], F32)
    CW = sbt("CW", [128, TT * 8], F32)
    CWS = sbt("CWS", [128, 8], F32)
    MEAN = sbt("MEAN", [128, 16], F32)
    B1C = sbt("B1C", [128, 32], F32)
    LNC = sbt("LNC", [128, 32], F32)
    BQC = sbt("BQC", [128, 16], F32)
    BRT = sbt("BRT", [32, 128], F32)
    BR32 = sbt("BR32", [1, 256], F32)
    BRB = sbt("BRB", [1, 3072], BF16)
    GB = sbt("GBrow", [1, NL * NEXP], F32)
    GW = sbt("GW", [128, NL * 64], F32)
    FLAGS = sbt("FLAGS", [1, 8], mybir.dt.int32)
    EYEB = sbt("EYEB", [128, 128], BF16)
    EYE32 = sbt("EYE32", [128, 128], F32)
    ONESB = sbt("ONESB", [1, 128], BF16)
    ONES32 = sbt("ONES32", [1, 128], F32)
    MSK = sbt("MSK", [128, 128], F32)

    PS_S = pst_("PS_S", [128, 2048], F32)
    PS_M0 = pst_("PS_M0", [128, 512], F32)
    PS_M1 = pst_("PS_M1", [128, 512], F32)
    PS_T0 = pst_("PS_T0", [128, 1024], BF16)
    PS_T1 = pst_("PS_T1", [128, 1024], BF16)
    PSM = [PS_M0, PS_M1]
    PST = [PS_T0, PS_T1]

    sems = {}
    for name in ("T", "V", "S", "Y", "P",
                 "dS0", "dS1", "dS2", "dS3", "dG0", "dG1", "dG2", "dG3"):
        sems[name] = es.enter_context(nc.semaphore(name))

    p = Prog()

    bS = [Slot() for _ in range(4)]
    bM = [Slot(), Slot()]
    bT = [Slot(), Slot()]

    a_w = [None] * TT
    a_r = [[] for _ in range(TT)]
    hf = Slot()
    cb = Slot()
    qb = Slot()
    kb = Slot()
    vb = Slot()
    pbh = [Slot(), Slot()]
    ptbh = [Slot(), Slot()]
    dum_s = Slot()
    nt = [Slot(), Slot()]
    nt32 = [Slot(), Slot()]
    h32 = [Slot(), Slot()]
    st_slots = {}
    cw_s = Slot()
    cws_s = Slot()
    mean_s = Slot()
    b1c_s = Slot()
    lnc_s = Slot()
    bqc_s = Slot()
    brt_s = Slot()
    br32_s = Slot()
    brb = [Slot(), Slot(), Slot(), Slot()]
    gw_s = Slot()
    gb_s = Slot()
    seq_gate_tok = [None]
    out_toks = []

    def stslot(kind, par):
        key = (kind, par)
        if key not in st_slots:
            st_slots[key] = Slot()
        return st_slots[key]

    def Asl(t):
        return A[:, t * 1024:(t + 1) * 1024]

    dma_rot = {"sync": [0], "gpsimd": [0]}

    def dma(queue, dst, src, deps=()):
        r = dma_rot[queue]
        sem = ("dS" if queue == "sync" else "dG") + str(r[0] % 4)
        r[0] += 1
        return p.op(queue, lambda e, d=dst, s=src: e.dma_start(out=d, in_=s),
                    deps=deps, sem=sem, amt=16)

    ws_cast = [None] * 2
    ws_i = [0]

    def stage(src_ap, dst_ap, dst_deps):
        s = ws_i[0] % 2
        ws_i[0] += 1
        st_ap = WS[:, s * 1024:(s + 1) * 1024]
        if len(src_ap.shape) == 3:
            st_view = st_ap.rearrange("p (c n) -> p c n", n=src_ap.shape[2])
        else:
            st_view = st_ap
        d = dma("sync", st_view, src_ap, deps=[ws_cast[s]])
        c = p.op("vector", lambda e, o=dst_ap, i_=st_ap: e.tensor_copy(o, i_),
                 deps=[d] + dst_deps)
        ws_cast[s] = c
        return c

    wb_users = [[] for _ in range(3)]
    wb_cast = [None] * 3
    wb_i = [0]

    def wload(src_ap):
        s = wb_i[0] % 3
        wb_i[0] += 1
        ap_ = WB[:, s * 1024:(s + 1) * 1024]
        ctok = stage(src_ap, ap_, wb_users[s] + [wb_cast[s]])
        wb_users[s] = []
        wb_cast[s] = ctok
        return s, ap_, ctok

    def wuse(s, tok):
        wb_users[s].append(tok)

    wb2_users = [[] for _ in range(NCH)]
    wb2_cast = [None] * NCH

    def w2load(j, src_ap):
        ap_ = WB2[:, j * 1024:(j + 1) * 1024]
        ctok = stage(src_ap, ap_, wb2_users[j] + [wb2_cast[j]])
        wb2_users[j] = []
        wb2_cast[j] = ctok
        return ap_, ctok

    def colslice(dram_cols):
        return dram_cols.rearrange("(c p) n -> p c n", p=128)


    def load_row_bf16(src_row, dst_off, slot):
        tok = None
        for hh_ in range(4):
            t1_ = dma("sync", BR32[:, :], src_row[:, hh_ * 256:(hh_ + 1) * 256],
                      deps=br32_s.for_write())
            br32_s.wrote(t1_)
            t2_ = p.op("vector",
                       lambda e, o=BRB[0:1, dst_off + hh_ * 256: dst_off + (hh_ + 1) * 256]:
                       e.tensor_copy(o, BR32[:, :]),
                       deps=[t1_] + (slot.for_write() if hh_ == 0 else []))
            br32_s.read(t2_)
            tok = t2_
        slot.wrote(tok)
        return tok

    CONSTS = sbt("CONSTS", [128, 2], F32)
    c0 = p.op("vector", lambda e: e.memset(CONSTS[:, 0:1], 0.0))
    c1 = p.op("vector", lambda e: e.memset(CONSTS[:, 1:2], EPS), deps=[c0])
    nc.const_aps.register(F32, 0.0, CONSTS[:, 0:1])
    nc.const_aps.register(F32, EPS, CONSTS[:, 1:2])
    p.const_tok = c1

    eyeb_tok = dma("sync", EYEB[:, :], d_eyeb.ap())
    eye32_tok = dma("sync", EYE32[:, :], d_eye32.ap())
    onesb_tok = dma("sync", ONESB[:, :], d_onesb.ap())
    ones32_tok = dma("sync", ONES32[:, :], d_ones32.ap())
    msk_tok = dma("sync", MSK[:, :], d_mask.ap())
    gb_s.wrote(dma("sync", GB[:, :], d_gb.ap()))
    gw_tok = None
    for l in range(NL):
        gw_tok = dma("sync",
                     GW[:, l * 64:(l + 1) * 64].rearrange("p (c n) -> p c n", n=NEXP),
                     d_gw.ap()[l * D:(l + 1) * D, :].rearrange("(c p) n -> p c n", p=128))
    gw_s.wrote(gw_tok)

    def ln_stats(t, src_ap, deps, par, banks=()):
        b = par * 8
        sl = stslot("ln", par)
        sdep = list(deps) + sl.for_write()
        s1 = ST[:, b:b + 1]
        s2 = ST[:, b + 1:b + 2]
        m = ST[:, b + 2:b + 3]
        nm2 = ST[:, b + 3:b + 4]
        var = ST[:, b + 4:b + 5]
        lnv = ST[:, b + 5:b + 6]
        rstd = ST[:, b + 6:b + 7]
        t_s1 = p.op("vector", lambda e: e.reduce_sum(s1, src_ap, AX), deps=sdep)
        t_s2 = p.op("scalar", lambda e: e.activation(DUM[:, :], src_ap, AF.Square,
                                                     accum_out=s2),
                    deps=sdep + dum_s.for_write() + ([t_s1] if banks else []))
        dum_s.wrote(t_s2)
        for bk in banks:
            bk.read(t_s1)
            bk.read(t_s2)
        t_m = p.op("vector", lambda e: e.tensor_scalar(m, s1, 1.0 / D, None, ALU.mult),
                   deps=[t_s1])
        t_nm2 = p.op("vector", lambda e: e.tensor_scalar(nm2, m, m, -1.0, ALU.mult, ALU.mult),
                     deps=[t_m])
        t_var = p.op("vector", lambda e: e.scalar_tensor_tensor(var, s2, 1.0 / D, nm2,
                                                                ALU.mult, ALU.add),
                     deps=[t_s2, t_nm2])
        t_ln = p.op("scalar", lambda e: e.activation(lnv, var, AF.Ln, bias=EPS, scale=1.0),
                    deps=[t_var])
        t_r = p.op("scalar", lambda e: e.activation(rstd, lnv, AF.Exp, scale=-0.5),
                   deps=[t_ln])
        sl.wrote(t_r)
        return m, rstd, t_r

    def emit_ln_to_hf(t, with_lnc):
        par = t % 2
        m, rstd, t_r = ln_stats(t, Asl(t), [a_w[t]], par)
        ntsl = nt[par]
        ntap = NT[:, par * 1024:(par + 1) * 1024]
        t_ap = p.op("vector",
                    lambda e, t=t, ntap=ntap, m=m, rstd=rstd: e.tensor_scalar(
                        ntap, Asl(t), m, rstd, ALU.subtract, ALU.mult),
                    deps=[a_w[t], t_r] + ntsl.for_write())
        a_r[t].append(t_ap)
        ntsl.wrote(t_ap)
        stslot("ln", par).read(t_ap)
        if with_lnc:
            wdeps2 = hf.for_write()
            lt = [None, None]
            for c in range(NCH):
                bk = c // 4
                pslice = PST[bk][:, (c % 4) * 128:(c % 4 + 1) * 128]
                lt[bk] = p.op("tensor",
                              lambda e, o=pslice, s=ntap[:, c * 128:(c + 1) * 128]:
                              e.transpose(o, s, EYEB[:, :]),
                              deps=[t_ap, eyeb_tok]
                              + (bT[bk].for_write() if c % 4 == 0 else []))
            bT[0].wrote(lt[0])
            bT[1].wrote(lt[1])
            et_sc = et_v = None
            for c in range(NCH):
                bk = c // 4
                pslice = PST[bk][:, (c % 4) * 128:(c % 4 + 1) * 128]
                hfdst = HF[:, c * T + t * 128: c * T + t * 128 + 128]
                if bk == 0:
                    et_sc = p.op("scalar",
                                 lambda e, o=hfdst, i_=pslice, c=c: e.activation(
                                     o, i_, AF.Identity,
                                     bias=LNC[:, 8 + c:9 + c], scale=LNC[:, c:c + 1]),
                                 deps=[lt[0], lnc_s.for_read()] + wdeps2)
                    lnc_s.read(et_sc)
                    bT[0].read(et_sc)
                else:
                    et_v = p.op("vector",
                                lambda e, o=hfdst, i_=pslice, c=c: e.tensor_scalar(
                                    o, i_, LNC[:, c:c + 1], LNC[:, 8 + c:9 + c],
                                    ALU.mult, ALU.add),
                                deps=[lt[1], lnc_s.for_read()] + wdeps2)
                    lnc_s.read(et_v)
                    bT[1].read(et_v)
            hf.wrote_multi([et_sc, et_v])
            et = et_sc
        else:
            bank = bT[t % 2]
            wdeps = bank.for_write()
            last_tt = None
            for c in range(NCH):
                pslice = PST[t % 2][:, c * 128:(c + 1) * 128]
                last_tt = p.op("tensor",
                               lambda e, o=pslice, s=ntap[:, c * 128:(c + 1) * 128]:
                               e.transpose(o, s, EYEB[:, :]),
                               deps=[t_ap, eyeb_tok] + (wdeps if c == 0 else []))
            bank.wrote(last_tt)
            hfv = HF[:, :].rearrange("p (c n) -> p c n", c=NCH)[:, :, t * 128: t * 128 + 128]
            psv = PST[t % 2][:, 0:NCH * 128].rearrange("p (c n) -> p c n", c=NCH)
            et = p.op("scalar",
                      lambda e, o=hfv, i_=psv: e.activation(o, i_, AF.Identity),
                      deps=[last_tt] + hf.for_write())
            bank.read(et)
            hf.wrote(et)
        ntsl.read(et)

    def emit_attention(l, outer):
        attn_idx = [0]
        attn_pend = []

        def emit_trans_av(ctx):
            hh, qt, g2, t_nm = ctx["hh"], ctx["qt"], ctx["g"], ctx["t_nm"]
            reg = ctx["reg"]
            regs = ctx["regs"]
            nkt = qt + 1
            GS4 = 4
            ngrp4 = (nkt + GS4 - 1) // GS4
            ev_toks = []
            for g4 in range(ngrp4):
                k0 = g4 * GS4
                cnt = min(GS4, nkt - k0)
                bank = bT[g4 % 2]
                slot0 = ((g4 // 2) % 2) * GS4
                wdeps = bank.for_write()
                last_tt = None
                for kt in range(k0, k0 + cnt):
                    pslice = PST[g4 % 2][:, (slot0 + kt - k0) * 128:(slot0 + kt - k0 + 1) * 128]
                    last_tt = p.op("tensor",
                                   lambda e, o=pslice, kt=kt: e.transpose(
                                       o, PB[:, reg + kt * 128: reg + (kt + 1) * 128],
                                       EYEB[:, :]),
                                   deps=[t_nm, eyeb_tok] + (wdeps if kt == k0 else []))
                    for r in regs:
                        pbh[r].read(last_tt)
                bank.wrote(last_tt)
                ptw = []
                for r in regs:
                    ptw += ptbh[r].for_write()
                et = p.op("vector" if g4 % 2 == 0 else "scalar",
                          lambda e, o=PTB[:, reg + k0 * 128: reg + (k0 + cnt) * 128],
                          i_=PST[g4 % 2][:, slot0 * 128:(slot0 + cnt) * 128]:
                          (e.copy(o, i_) if hasattr(e, "copy") else e.tensor_copy(o, i_)),
                          deps=[last_tt] + ptw)
                bank.read(et)
                for r in regs:
                    ptbh[r].wrote(et)
                ev_toks.append(et)
            ai = attn_idx[0]
            avbank = bM[ai % 2]
            avdeps = [vb.for_read()] + avbank.for_write()
            last = None
            for kt in range(nkt):
                last = p.op("tensor",
                            lambda e, kt=kt, hh=hh, qt=qt, ai=ai, st=(kt == 0):
                            e.matmul(PSM[ai % 2][hh * 64:(hh + 1) * 64, 0:128],
                                     VB[:, kt * 128 + hh * 64: kt * 128 + (hh + 1) * 64],
                                     PTB[:, reg + kt * 128: reg + (kt + 1) * 128],
                                     start=st, stop=(kt == qt)),
                            deps=([ev_toks[kt // GS4]] + (avdeps if kt == 0 else [])))
            avbank.wrote(last)
            for r in regs:
                ptbh[r].read(last)
            vb.read(last)
            yt = p.op("scalar",
                      lambda e, hh=hh, qt=qt, g2=g2, ai=ai:
                      e.activation(CB[hh * 64:(hh + 1) * 64,
                                      g2 * T + qt * 128: g2 * T + qt * 128 + 128],
                                   PSM[ai % 2][hh * 64:(hh + 1) * 64, 0:128],
                                   AF.Identity),
                      deps=[last] + cb.for_write())
            avbank.read(yt)
            cb.wrote(yt)

        for g in range(NG):
            if outer:
                sq, wq_ap, wq_tok = wload(colslice(d_wqkv.ap()[:, g * 128:(g + 1) * 128]))
                sk, wk_ap, wk_tok = wload(colslice(d_wqkv.ap()[:, D + g * 128: D + (g + 1) * 128]))
                sv, wv_ap, wv_tok = wload(colslice(d_wqkv.ap()[:, 2 * D + g * 128: 2 * D + (g + 1) * 128]))
                wo_ap, wo_tok = w2load(g, d_wout.ap()[g * 128:(g + 1) * 128, :])
            else:
                sq, wq_ap, wq_tok = wload(colslice(d_wq.ap()[l * D:(l + 1) * D, :][:, g * 128:(g + 1) * 128]))
                sk, wk_ap, wk_tok = wload(colslice(d_wk.ap()[l * D:(l + 1) * D, :][:, g * 128:(g + 1) * 128]))
                sv, wv_ap, wv_tok = wload(colslice(d_wv.ap()[l * D:(l + 1) * D, :][:, g * 128:(g + 1) * 128]))
                wo_ap, wo_tok = w2load(g, d_wo.ap()[l * D + g * 128: l * D + (g + 1) * 128, :])
            if g == 0:
                wo_all = []
            wo_all.append((g, wo_ap, wo_tok))

            for w_s, w_ap, w_tok, dst, dsts, bias_col in (
                (sq, wq_ap, wq_tok, QB, qb, (None if outer else BQC[:, g:g + 1])),
                (sk, wk_ap, wk_tok, KBf, kb, (None if outer else BQC[:, 8 + g:9 + g])),
            ):
                for grp in range(NGRP):
                    bank = bM[grp % 2]
                    mmdeps = [w_tok, hf.for_read()] + bank.for_write()
                    last = None
                    for c in range(NCH):
                        last = p.op("tensor",
                                    lambda e, c=c, w=w_ap, gp=grp, st=(c == 0):
                                    e.matmul(PSM[gp % 2][:, :],
                                             w[:, c * 128:(c + 1) * 128],
                                             HF[:, c * T + gp * 512: c * T + gp * 512 + 512],
                                             start=st, stop=(c == NCH - 1)),
                                    deps=(mmdeps if c == 0 else [w_tok]))
                    bank.wrote(last)
                    hf.read(last)
                    wuse(w_s, last)
                    sc_ = SCALE if dst is QB else 1.0
                    if bias_col is None:
                        et = p.op("scalar",
                                  lambda e, o=dst[:, grp * 512:(grp + 1) * 512],
                                  i_=PSM[grp % 2][:, :], sc_=sc_:
                                  e.activation(o, i_, AF.Identity, scale=sc_),
                                  deps=[last] + dsts.for_write())
                    else:
                        et = p.op("scalar",
                                  lambda e, o=dst[:, grp * 512:(grp + 1) * 512],
                                  i_=PSM[grp % 2][:, :], bi=bias_col, sc_=sc_:
                                  e.activation(o, i_, AF.Identity, bias=bi, scale=sc_),
                                  deps=[last, bqc_s.for_read()] + dsts.for_write())
                        bqc_s.read(et)
                    bank.read(et)
                    dsts.wrote(et)

            for t in range(TT):
                bank = bM[t % 2]
                mmdeps = [wv_tok, hf.for_read()] + bank.for_write()
                last = None
                for c in range(NCH):
                    last = p.op("tensor",
                                lambda e, c=c, t=t, w=wv_ap, st=(c == 0),
                                fin=(c == NCH - 1 and outer):
                                e.matmul(PSM[t % 2][:, 0:128],
                                         HF[:, c * T + t * 128: c * T + t * 128 + 128],
                                         w[:, c * 128:(c + 1) * 128],
                                         start=st, stop=fin),
                                deps=(mmdeps if c == 0 else [wv_tok]))
                hf.read(last)
                if not outer:
                    last = p.op("tensor",
                                lambda e, t=t, g=g: e.matmul(
                                    PSM[t % 2][:, 0:128],
                                    ONESB[0:1, 0:128],
                                    BRB[0:1, 2048 + g * 128: 2048 + (g + 1) * 128],
                                    start=False, stop=True),
                                deps=[onesb_tok, brb[3].for_read()])
                    brb[3].read(last)
                bank.wrote(last)
                wuse(sv, last)
                et = p.op("scalar",
                          lambda e, t=t: e.activation(VB[:, t * 128:(t + 1) * 128],
                                                      PSM[t % 2][:, 0:128], AF.Identity),
                          deps=[last] + vb.for_write())
                bank.read(et)
                vb.wrote(et)

            for hh in range(2):
                for qt in range(TT):
                    klen = (qt + 1) * 128
                    n512 = (klen + 511) // 512
                    short = klen <= min(1024, T // 2)
                    base = 2 * (attn_idx[0] % 2) if short else 0
                    off = base * 512
                    par = attn_idx[0] % 2
                    b = 16 + par * 8
                    asl = stslot("att", par)
                    mx = ST[:, b:b + 1]
                    ngm = ST[:, b + 1:b + 2]
                    sm = ST[:, b + 2:b + 3]
                    rc = ST[:, b + 3:b + 4]
                    cmx = ST[:, b + 4:b + 8]
                    bank_toks = []
                    mx_toks = []
                    sdeps = [qb.for_read(), kb.for_read()] + asl.for_write()
                    for c5 in range(n512):
                        nn = min(512, klen - c5 * 512)
                        bk = bS[base + c5]
                        mt = p.op("tensor",
                                  lambda e, c5=c5, nn=nn, hh=hh, qt=qt, off=off:
                                  e.matmul(PS_S[:, off + c5 * 512: off + c5 * 512 + nn],
                                           QB[hh * 64:(hh + 1) * 64, qt * 128:(qt + 1) * 128],
                                           KBf[hh * 64:(hh + 1) * 64, c5 * 512: c5 * 512 + nn],
                                           start=True, stop=True),
                                  deps=sdeps + bk.for_write())
                        bk.wrote(mt)
                        bank_toks.append(mt)
                        if c5 == n512 - 1:
                            mk = p.op("vector",
                                      lambda e, qt=qt, off=off: e.tensor_tensor(
                                          PS_S[:, off + qt * 128: off + (qt + 1) * 128],
                                          PS_S[:, off + qt * 128: off + (qt + 1) * 128],
                                          MSK[:, :], ALU.add),
                                      deps=[msk_tok, mt] + bk.for_write())
                            bk.wrote(mk)
                            tmx = p.op("vector",
                                       lambda e, c5=c5, nn=nn, off=off: e.reduce_max(
                                           cmx[:, c5:c5 + 1],
                                           PS_S[:, off + c5 * 512: off + c5 * 512 + nn], AX),
                                       deps=[mk])
                        else:
                            tmx = p.op("vector",
                                       lambda e, c5=c5, nn=nn, off=off: e.reduce_max(
                                           cmx[:, c5:c5 + 1],
                                           PS_S[:, off + c5 * 512: off + c5 * 512 + nn], AX),
                                       deps=[mt])
                        bk.read(tmx)
                        mx_toks.append(tmx)
                    qb.read(bank_toks[-1])
                    kb.read(bank_toks[-1])

                    regs = ((attn_idx[0] % 2),) if short else (0, 1)
                    reg = (attn_idx[0] % 2) * (T // 2) if short else 0
                    keep = []
                    for pd in attn_pend:
                        if set(pd["regs"]) & set(regs):
                            emit_trans_av(pd)
                        else:
                            keep.append(pd)
                    attn_pend[:] = keep

                    attn_idx[0] += 1
                    t_ng = p.op("vector",
                                lambda e, n512=n512, ngm=ngm: e.reduce_max(
                                    ngm, cmx[:, 0:n512], AX, negate=True),
                                deps=mx_toks)
                    pbw = []
                    for r in regs:
                        pbw += pbh[r].for_write()
                    t_ex = p.op("scalar",
                                lambda e, klen=klen, off=off, reg=reg, ngm=ngm, sm=sm:
                                e.activation(
                                    PB[:, reg:reg + klen], PS_S[:, off:off + klen],
                                    AF.Exp, bias=ngm, scale=1.0, accum_out=sm),
                                deps=[t_ng] + bank_toks + pbw)
                    for c5 in range(n512):
                        bS[base + c5].read(t_ex)
                    for r in regs:
                        pbh[r].wrote(t_ex)
                    t_rc = p.op("vector", lambda e, rc=rc, sm=sm: e.reciprocal(rc, sm),
                                deps=[t_ex])
                    t_nm = p.op("vector",
                                lambda e, klen=klen, reg=reg, rc=rc: e.tensor_scalar(
                                    PB[:, reg:reg + klen], PB[:, reg:reg + klen],
                                    rc, None, ALU.mult),
                                deps=[t_rc, t_ex])
                    for r in regs:
                        pbh[r].wrote(t_nm)
                    asl.wrote(t_nm)
                    attn_pend.append({"hh": hh, "qt": qt, "g": g, "t_nm": t_nm,
                                      "reg": reg, "regs": regs})

            for pd in attn_pend:
                emit_trans_av(pd)
            attn_pend[:] = []

        w2cast_last = wo_all[-1][2]
        for t in range(TT):
            half = t % 2
            off = half * 1024
            banks = [bS[half * 2], bS[half * 2 + 1]]
            mmdeps = ([cb.for_read(), w2cast_last]
                      + banks[0].for_write() + banks[1].for_write())
            last = None
            for g, wo_ap, wo_tok in wo_all:
                for piece in range(2):
                    last = p.op("tensor",
                                lambda e, g=g, piece=piece, t=t, off=off, wo_ap=wo_ap,
                                st=(g == 0 and piece == 0):
                                e.matmul(PS_S[:, off + piece * 512: off + piece * 512 + 512],
                                         CB[:, g * T + t * 128: g * T + t * 128 + 128],
                                         wo_ap[:, piece * 512:(piece + 1) * 512],
                                         start=(g == 0), stop=False),
                                deps=(mmdeps if (g == 0 and piece == 0) else []))
            cb.read(last)
            rowslot = brb[1]
            rowoff = 1024
            for piece in range(2):
                last = p.op("tensor",
                            lambda e, piece=piece, off=off, rowoff=rowoff:
                            e.matmul(PS_S[:, off + piece * 512: off + piece * 512 + 512],
                                     ONESB[0:1, 0:128],
                                     BRB[0:1, rowoff + piece * 512: rowoff + (piece + 1) * 512],
                                     start=False, stop=True),
                            deps=[onesb_tok, rowslot.for_read()])
            rowslot.read(last)
            banks[0].wrote(last)
            banks[1].wrote(last)
            psap = PS_S[:, off:off + 1024]
            if outer:
                par = t % 2
                m, rstd, t_r = ln_stats(t, psap, [last], par, banks=banks)
                tmp = NT32[:, par * 1024:(par + 1) * 1024]
                t_ap2 = p.op("vector",
                             lambda e, tmp=tmp, psap=psap, m=m, rstd=rstd:
                             e.tensor_scalar(tmp, psap, m, rstd, ALU.subtract, ALU.mult),
                             deps=[t_r, last] + nt32[par].for_write())
                banks[0].read(t_ap2)
                banks[1].read(t_ap2)
                nt32[par].wrote(t_ap2)
                stslot("ln", par).read(t_ap2)
                t_add = p.op("vector",
                             lambda e, t=t, tmp=tmp: e.tensor_tensor(
                                 Asl(t), Asl(t), tmp, ALU.add),
                             deps=[t_ap2, a_w[t]] + a_r[t])
                nt32[par].read(t_add)
                a_w[t] = t_add
                a_r[t] = []
            else:
                t_add = p.op("vector",
                             lambda e, t=t, psap=psap: e.scalar_tensor_tensor(
                                 Asl(t), psap, 1.0, Asl(t), ALU.mult, ALU.add),
                             deps=[last, a_w[t]] + a_r[t])
                banks[0].read(t_add)
                banks[1].read(t_add)
                a_w[t] = t_add
                a_r[t] = []
        for g, wo_ap, wo_tok in wo_all:
            wb2_users[g].append(last)

    def emit_topk(logits_ap, ldeps, cw_ap, cw_slot, par, nrows):
        b = 32 + par * 16
        gb_ = par * 32
        sl = stslot("gate", par)
        gsl = stslot("gs", par)
        m1 = ST[0:nrows, b:b + 1]
        nm1 = ST[0:nrows, b + 1:b + 2]
        m2 = ST[0:nrows, b + 2:b + 3]
        e2c = ST[0:nrows, b + 3:b + 4]
        den = ST[0:nrows, b + 4:b + 5]
        rr = ST[0:nrows, b + 5:b + 6]
        eq1 = GS[0:nrows, gb_:gb_ + 8]
        l2 = GS[0:nrows, gb_ + 8:gb_ + 16]
        eq2 = GS[0:nrows, gb_ + 16:gb_ + 24]
        tt1 = GS[0:nrows, gb_ + 24:gb_ + 32]
        d0 = list(ldeps) + sl.for_write() + gsl.for_write()
        t1 = p.op("vector", lambda e: e.reduce_max(m1, logits_ap, AX), deps=d0)
        t2 = p.op("vector", lambda e: e.tensor_scalar(eq1, logits_ap, m1, None,
                                                      ALU.is_equal),
                  deps=[t1] + list(ldeps))
        t3 = p.op("vector", lambda e: e.scalar_tensor_tensor(l2, eq1, BIGNEG, logits_ap,
                                                             ALU.mult, ALU.add),
                  deps=[t2] + list(ldeps))
        t4 = p.op("vector", lambda e: e.reduce_max(m2, l2, AX), deps=[t3])
        t5 = p.op("vector", lambda e: e.tensor_scalar(nm1, m1, -1.0, None, ALU.mult),
                  deps=[t1])
        t6 = p.op("scalar", lambda e: e.activation(e2c, m2, AF.Exp, bias=nm1),
                  deps=[t4, t5])
        t7 = p.op("vector", lambda e: e.tensor_scalar(den, e2c, 1.0, None, ALU.add),
                  deps=[t6])
        t8 = p.op("vector", lambda e: e.reciprocal(rr, den), deps=[t7])
        t9 = p.op("vector", lambda e: e.tensor_scalar(eq2, l2, m2, None, ALU.is_equal),
                  deps=[t4, t3])
        t10 = p.op("vector", lambda e: e.scalar_tensor_tensor(tt1, eq2, e2c, eq1,
                                                              ALU.mult, ALU.add),
                   deps=[t9, t6, t2])
        t11 = p.op("vector", lambda e: e.tensor_scalar(cw_ap, tt1, rr, None, ALU.mult),
                   deps=[t10, t8] + cw_slot.for_write())
        cw_slot.wrote(t11)
        sl.wrote(t11)
        gsl.wrote(t11)
        return t11

    def emit_ln2_gate(l):
        if l == 1:
            tz = p.op("vector", lambda e: e.memset(MEAN[:, 0:9], 0.0),
                      deps=mean_s.for_write())
            mean_s.wrote(tz)
        for t in range(TT):
            par = t % 2
            m, rstd, t_r = ln_stats(t, Asl(t), [a_w[t]], par)
            ntap = NT32[:, par * 1024:(par + 1) * 1024]
            t_ap = p.op("vector",
                        lambda e, t=t, ntap=ntap, m=m, rstd=rstd: e.tensor_scalar(
                            ntap, Asl(t), m, rstd, ALU.subtract, ALU.mult),
                        deps=[a_w[t], t_r] + nt32[par].for_write())
            a_r[t].append(t_ap)
            nt32[par].wrote(t_ap)
            stslot("ln", par).read(t_ap)
            gate_last = None
            for c in range(NCH):
                i = t * NCH + c
                bank = bM[i % 2]
                pslice = PSM[i % 2][:, 0:128]
                tt = p.op("tensor",
                          lambda e, o=pslice, ntap=ntap, c=c: e.transpose(
                              o, ntap[:, c * 128:(c + 1) * 128], EYE32[:, :]),
                          deps=[t_ap, eye32_tok] + bank.for_write())
                bank.wrote(tt)
                nt32[par].read(tt)
                e1 = p.op("scalar",
                          lambda e, o=HF[:, c * T + t * 128: c * T + t * 128 + 128],
                          i_=pslice, c=c: e.activation(o, i_, AF.Identity,
                                                       bias=LNC[:, 24 + c:25 + c],
                                                       scale=LNC[:, 16 + c:17 + c]),
                          deps=[tt, lnc_s.for_read()] + hf.for_write())
                hf.wrote(e1)
                lnc_s.read(e1)
                bank.read(e1)
                hsl = h32[i % 2]
                e2 = p.op("vector",
                          lambda e, i_=pslice, c=c, sl=(i % 2): e.tensor_scalar(
                              H32[:, sl * 128:(sl + 1) * 128], i_,
                              LNC[:, 16 + c:17 + c], LNC[:, 24 + c:25 + c],
                              ALU.mult, ALU.add),
                          deps=[tt, e1, lnc_s.for_read()] + hsl.for_write())
                lnc_s.read(e2)
                bank.read(e2)
                hsl.wrote(e2)
                if l == 0:
                    gm = p.op("tensor",
                              lambda e, t=t, c=c, sl=(i % 2), l=l, st=(c == 0):
                              e.matmul(PS_S[:, t * 8:(t + 1) * 8],
                                       H32[:, sl * 128:(sl + 1) * 128],
                                       GW[:, l * 64 + c * 8: l * 64 + (c + 1) * 8],
                                       start=st, stop=False),
                              deps=[e2, gw_s.for_read()]
                              + (bS[0].for_write() if c == 0 else []))
                    hsl.read(gm)
                    gw_s.read(gm)
                    gate_last = gm
                else:
                    tcol = MEAN[:, 8:9]
                    tr_ = p.op("vector",
                               lambda e, sl=(i % 2), tcol=tcol: e.reduce_sum(
                                   tcol, H32[:, sl * 128:(sl + 1) * 128], AX),
                               deps=[e2] + mean_s.for_write())
                    mean_s.wrote(tr_)
                    hsl.read(tr_)
                    ta_ = p.op("vector",
                               lambda e, c=c, tcol=tcol: e.tensor_tensor(
                                   MEAN[:, c:c + 1], MEAN[:, c:c + 1], tcol, ALU.add),
                               deps=[tr_] + mean_s.for_write())
                    mean_s.wrote(ta_)
            if l == 0:
                gl = p.op("tensor",
                          lambda e, t=t, l=l: e.matmul(
                              PS_S[:, t * 8:(t + 1) * 8], ONES32[0:1, 0:128],
                              GB[0:1, l * 8:(l + 1) * 8],
                              start=False, stop=True),
                          deps=[ones32_tok, gb_s.for_read(), gate_last])
                gb_s.read(gl)
                bS[0].wrote(gl)
                lg = GS[:, 64 + (t % 2) * 8: 64 + (t % 2) * 8 + 8]
                lgsl = stslot("lg", t % 2)
                t_cp = p.op("vector",
                            lambda e, t=t, lg=lg: e.tensor_copy(
                                lg, PS_S[:, t * 8:(t + 1) * 8]),
                            deps=[gl] + lgsl.for_write())
                bS[0].read(t_cp)
                lgsl.wrote(t_cp)
                tk = emit_topk(lg, [t_cp], CW[:, t * 8:(t + 1) * 8], cw_s, t % 2,
                               nrows=128)
                lgsl.read(tk)

    def emit_seq_gate(l):
        tsc = p.op("vector",
                   lambda e: e.tensor_scalar(MEAN[:, 0:8], MEAN[:, 0:8], 1.0 / T, None,
                                             ALU.mult),
                   deps=mean_s.for_write())
        mean_s.wrote(tsc)
        bank = bS[0]
        last = None
        for c in range(NCH):
            last = p.op("tensor",
                        lambda e, c=c, l=l, st=(c == 0): e.matmul(
                            PS_S[0:1, 0:8], MEAN[:, c:c + 1],
                            GW[:, l * 64 + c * 8: l * 64 + (c + 1) * 8],
                            start=st, stop=False),
                        deps=[tsc, gw_s.for_read()] + (bank.for_write() if c == 0 else []))
            gw_s.read(last)
        mean_s.read(last)
        gl = p.op("tensor",
                  lambda e, l=l: e.matmul(PS_S[0:1, 0:8], ONES32[0:1, 0:1],
                                          GB[0:1, l * 8:(l + 1) * 8],
                                          start=False, stop=True),
                  deps=[ones32_tok, gb_s.for_read(), last])
        gb_s.read(gl)
        bank.wrote(gl)
        lg = GS[0:1, 72:80]
        lgsl = stslot("lgrow", 0)
        t_cp = p.op("vector", lambda e: e.tensor_copy(lg, PS_S[0:1, 0:8]),
                    deps=[gl] + lgsl.for_write())
        bank.read(t_cp)
        lgsl.wrote(t_cp)
        trow = emit_topk(lg, [t_cp], GS[0:1, 48:56], stslot("cwrow", 0), 0, nrows=1)
        lgsl.read(trow)
        tfl = p.op("vector",
                   lambda e: e.tensor_single_scalar(FLAGS[0:1, :], GS[0:1, 48:56],
                                                    0.0, ALU.is_gt),
                   deps=[trow])
        seq_gate_tok[0] = tfl
        bb = p.op("tensor",
                  lambda e: e.matmul(PS_S[:, 8:16], ONES32[0:1, 0:128], GS[0:1, 48:56],
                                     start=True, stop=True),
                  deps=[trow, ones32_tok] + bank.for_write())
        stslot("cwrow", 0).read(bb)
        bank.wrote(bb)
        tc = p.op("vector", lambda e: e.tensor_copy(CWS[:, 0:8], PS_S[:, 8:16]),
                  deps=[bb] + cws_s.for_write())
        bank.read(tc)
        cws_s.wrote(tc)

    def emit_moe(l):
        for ex in range(NEXP):
            if l == 1:
                p.cond_ctx = (("m1", ex), FLAGS[0:1, ex:ex + 1], seq_gate_tok[0])
            tb = dma("sync", BRT[:, :],
                     d_eb1.ap()[(l * NEXP + ex) * 32:(l * NEXP + ex + 1) * 32, :],
                     deps=brt_s.for_write())
            brt_s.wrote(tb)
            bank = bM[0]
            tt = p.op("tensor",
                      lambda e: e.transpose(PSM[0][:, 0:32], BRT[:, :], EYE32[0:32, 0:32]),
                      deps=[tb, eye32_tok] + bank.for_write())
            bank.wrote(tt)
            brt_s.read(tt)
            tc = p.op("vector", lambda e: e.tensor_copy(B1C[:, :], PSM[0][:, 0:32]),
                      deps=[tt] + b1c_s.for_write())
            bank.read(tc)
            b1c_s.wrote(tc)
            load_row_bf16(d_eb2.ap()[l * NEXP + ex: l * NEXP + ex + 1, :], 0, brb[0])

            for ph in range(NPHASE):
                last_gelu = None
                w2cast_last = None
                for j in range(JCH):
                    J = ph * JCH + j
                    rowbase = (l * NEXP + ex) * DFF + J * 128
                    w2_ap, w2_tok = w2load(j, d_ew2.ap()[rowbase: rowbase + 128, :])
                    w2cast_last = w2_tok
                    colbase = (l * NEXP + ex) * D
                    s1_, w1_ap, w1_tok = wload(
                        colslice(d_ew1.ap()[colbase: colbase + D, :][:, J * 128:(J + 1) * 128]))
                    for grp in range(NGRP):
                        bank = bM[(j * NGRP + grp) % 2]
                        mmdeps = [w1_tok, hf.for_read()] + bank.for_write()
                        last = None
                        for c in range(NCH):
                            last = p.op("tensor",
                                        lambda e, c=c, grp=grp, w=w1_ap,
                                        bk=((j * NGRP + grp) % 2), st=(c == 0):
                                        e.matmul(PSM[bk][:, :],
                                                 w[:, c * 128:(c + 1) * 128],
                                                 HF[:, c * T + grp * 512: c * T + grp * 512 + 512],
                                                 start=st, stop=(c == NCH - 1)),
                                        deps=(mmdeps if c == 0 else [w1_tok]))
                        bank.wrote(last)
                        hf.read(last)
                        wuse(s1_, last)
                        gt = p.op("scalar",
                                  lambda e, j=j, grp=grp, bk=((j * NGRP + grp) % 2), J=J:
                                  e.activation(CB[:, j * T + grp * 512: j * T + grp * 512 + 512],
                                               PSM[bk][:, :], AF.Gelu,
                                               bias=B1C[:, J:J + 1]),
                                  deps=[last, tc] + cb.for_write())
                        bank.read(gt)
                        cb.wrote(gt)
                        last_gelu = gt
                b1c_s.read(last_gelu)
                last_mm2 = None
                for t in range(TT):
                    half = t % 2
                    off = half * 1024
                    banks = [bS[half * 2], bS[half * 2 + 1]]
                    mmdeps = ([cb.for_read(), last_gelu, w2cast_last]
                              + banks[0].for_write() + banks[1].for_write())
                    last = None
                    for j in range(JCH):
                        for piece in range(2):
                            fin = (ph > 0 and j == JCH - 1)
                            last = p.op("tensor",
                                        lambda e, j=j, piece=piece, t=t, off=off, fin=fin:
                                        e.matmul(PS_S[:, off + piece * 512: off + piece * 512 + 512],
                                                 CB[:, j * T + t * 128: j * T + t * 128 + 128],
                                                 WB2[:, j * 1024 + piece * 512:
                                                     j * 1024 + (piece + 1) * 512],
                                                 start=(j == 0), stop=fin),
                                        deps=(mmdeps if (j == 0 and piece == 0) else []))
                    if ph == 0:
                        for piece in range(2):
                            last = p.op("tensor",
                                        lambda e, piece=piece, off=off:
                                        e.matmul(PS_S[:, off + piece * 512: off + piece * 512 + 512],
                                                 ONESB[0:1, 0:128],
                                                 BRB[0:1, piece * 512:(piece + 1) * 512],
                                                 start=False, stop=True),
                                        deps=[onesb_tok, brb[0].for_read()])
                        brb[0].read(last)
                    banks[0].wrote(last)
                    banks[1].wrote(last)
                    cb.read(last)
                    last_mm2 = last
                    col = (CW[:, t * 8 + ex: t * 8 + ex + 1] if l == 0
                           else CWS[:, ex:ex + 1])
                    cwdep = (cw_s if l == 0 else cws_s).for_read()
                    t_add = p.op("vector",
                                 lambda e, t=t, off=off, col=col: e.scalar_tensor_tensor(
                                     Asl(t), PS_S[:, off:off + 1024], col, Asl(t),
                                     ALU.mult, ALU.add),
                                 deps=[last, a_w[t]] + cwdep + a_r[t])
                    banks[0].read(t_add)
                    banks[1].read(t_add)
                    a_w[t] = t_add
                    a_r[t] = []
                for j in range(JCH):
                    wb2_users[j].append(last_mm2)
            p.cond_ctx = None

    def emit_final(seq):
        for t in range(TT):
            par = t % 2
            m, rstd, t_r = ln_stats(t, Asl(t), [a_w[t]], par)
            ntap = NT32[:, par * 1024:(par + 1) * 1024]
            t_ap = p.op("vector",
                        lambda e, t=t, ntap=ntap, m=m, rstd=rstd: e.tensor_scalar(
                            ntap, Asl(t), m, rstd, ALU.subtract, ALU.mult),
                        deps=[a_w[t], t_r] + nt32[par].for_write())
            a_r[t].append(t_ap)
            nt32[par].wrote(t_ap)
            stslot("ln", par).read(t_ap)
            t_ld = dma("gpsimd", Asl(t), d_xsave.ap()[t * 128:(t + 1) * 128, :],
                       deps=[t_ap, a_w[t]] + a_r[t])
            t_add = p.op("vector",
                         lambda e, t=t, ntap=ntap: e.tensor_tensor(
                             Asl(t), Asl(t), ntap, ALU.add),
                         deps=[t_ap, t_ld])
            nt32[par].read(t_add)
            a_w[t] = t_add
            a_r[t] = []
            t_st = dma("gpsimd", d_out.ap()[seq * T + t * 128: seq * T + (t + 1) * 128, :],
                       Asl(t), deps=[t_add])
            a_r[t].append(t_st)
            out_toks.append(t_st)

    def emit_debug_out(seq):
        for t in range(TT):
            t_st = dma("gpsimd", d_out.ap()[seq * T + t * 128: seq * T + (t + 1) * 128, :],
                       Asl(t), deps=[a_w[t]] + a_r[t])
            a_r[t].append(t_st)
            out_toks.append(t_st)

    stopped = False
    for seq in range(n_seq):
        if stopped:
            break
        xdeps = []
        for t in range(TT):
            xdeps.append(a_w[t])
            xdeps.extend(a_r[t])
        xt = dma("sync", A[:, :].rearrange("p (t d) -> p t d", d=1024),
                 d_x.ap()[seq * T:(seq + 1) * T, :].rearrange("(t p) d -> p t d", p=128),
                 deps=xdeps)
        for t in range(TT):
            a_w[t] = xt
            a_r[t] = []
        if debug_stop == "x0":
            emit_debug_out(seq)
            continue

        load_row_bf16(d_bout.ap(), 1024, brb[1])
        for t in range(TT):
            emit_ln_to_hf(t, with_lnc=False)
        emit_attention(0, outer=True)
        if debug_stop == "x1":
            emit_debug_out(seq)
            continue

        for t in range(TT):
            t_sp = dma("gpsimd", d_xsave.ap()[t * 128:(t + 1) * 128, :], Asl(t),
                       deps=[a_w[t]] + a_r[t])
            a_r[t].append(t_sp)
            par = t % 2
            m, rstd, t_r = ln_stats(t, Asl(t), [a_w[t]], par)
            t_ap = p.op("vector",
                        lambda e, t=t, m=m, rstd=rstd: e.tensor_scalar(
                            Asl(t), Asl(t), m, rstd, ALU.subtract, ALU.mult),
                        deps=[t_r, t_sp, a_w[t]] + a_r[t])
            stslot("ln", par).read(t_ap)
            a_w[t] = t_ap
            a_r[t] = []
        if debug_stop == "y0":
            emit_debug_out(seq)
            continue

        for l in range(NL):
            tb = dma("sync", BRT[:, :], d_lnstack.ap()[l * 32:(l + 1) * 32, :],
                     deps=brt_s.for_write())
            brt_s.wrote(tb)
            bank = bM[1]
            tt1_ = p.op("tensor",
                        lambda e: e.transpose(PSM[1][:, 0:32], BRT[:, :], EYE32[0:32, 0:32]),
                        deps=[tb, eye32_tok] + bank.for_write())
            bank.wrote(tt1_)
            brt_s.read(tt1_)
            tc1 = p.op("vector", lambda e: e.tensor_copy(LNC[:, :], PSM[1][:, 0:32]),
                       deps=[tt1_] + lnc_s.for_write())
            bank.read(tc1)
            lnc_s.wrote(tc1)
            tb2 = dma("sync", BRT[0:16, :], d_bqk.ap()[l * 16:(l + 1) * 16, :],
                      deps=brt_s.for_write())
            brt_s.wrote(tb2)
            tt2_ = p.op("tensor",
                        lambda e: e.transpose(PSM[1][:, 64:80], BRT[0:16, :],
                                              EYE32[0:16, 0:16]),
                        deps=[tb2, eye32_tok] + bank.for_write())
            bank.wrote(tt2_)
            brt_s.read(tt2_)
            tc2 = p.op("vector", lambda e: e.tensor_copy(BQC[:, :], PSM[1][:, 64:80]),
                       deps=[tt2_] + bqc_s.for_write())
            bank.read(tc2)
            tc2b = p.op("vector",
                        lambda e: e.tensor_scalar(BQC[:, 0:8], BQC[:, 0:8], SCALE, None,
                                                  ALU.mult),
                        deps=[tc2])
            bqc_s.wrote(tc2b)
            load_row_bf16(d_bv.ap()[l:l + 1, :], 2048, brb[3])
            load_row_bf16(d_bo.ap()[l:l + 1, :], 1024, brb[1])

            for t in range(TT):
                emit_ln_to_hf(t, with_lnc=True)
            emit_attention(l, outer=False)
            if debug_stop == ("l0attn" if l == 0 else "l1attn"):
                emit_debug_out(seq)
                stopped = True
                break
            emit_ln2_gate(l)
            if l == 1:
                emit_seq_gate(l)
            emit_moe(l)
            if debug_stop == "l0" and l == 0:
                emit_debug_out(seq)
                stopped = True
                break
        if stopped:
            break
        if debug_stop == "y4":
            emit_debug_out(seq)
            continue
        emit_final(seq)

    p.op("gpsimd", lambda e: e.nop(), deps=out_toks)

    p.finalize_and_emit(es.enter_context(nc.Block()), sems)
    es.close()
    return nc, p


def _host_inputs(inputs, n_cores=8, t_tiles=16, n_seq=2):
    f32 = np.float32
    g = lambda k: np.ascontiguousarray(np.asarray(inputs[k], dtype=f32))
    T = t_tiles * 128
    x = g("L0_attn_x")
    base = {
        "wqkv": g("L0_attn_w_qkv"),
        "wout": g("L0_attn_w_out"),
        "bout": g("L0_attn_b_out").reshape(1, D),
        "wq": g("L0_ffn_wq").reshape(NL * D, D),
        "wk": g("L0_ffn_wk").reshape(NL * D, D),
        "wv": g("L0_ffn_wv").reshape(NL * D, D),
        "wo": g("L0_ffn_wo").reshape(NL * D, D),
        "lnstack": np.ascontiguousarray(
            np.stack([g("L0_ffn_ln1_w"), g("L0_ffn_ln1_b"),
                      g("L0_ffn_ln2_w"), g("L0_ffn_ln2_b")],
                     axis=1).reshape(NL * 32, 128)),
        "bqk": np.ascontiguousarray(
            np.stack([g("L0_ffn_bq"), g("L0_ffn_bk")], axis=1).reshape(NL * 16, 128)),
        "bv": g("L0_ffn_bv"),
        "bo": g("L0_ffn_bo"),
        "gatew": g("L0_ffn_gate_w").reshape(NL * D, NEXP),
        "gateb": g("L0_ffn_gate_b").reshape(1, NL * NEXP),
        "ew1": g("L0_ffn_e_w1").reshape(NL * NEXP * D, DFF),
        "eb1": g("L0_ffn_e_b1").reshape(NL * NEXP * 32, 128),
        "ew2": g("L0_ffn_e_w2").reshape(NL * NEXP * DFF, D),
        "eb2": g("L0_ffn_e_b2").reshape(NL * NEXP, D),
        "eyeb": np.eye(128, dtype=ml_dtypes.bfloat16),
        "eye32": np.eye(128, dtype=f32),
        "onesb": np.ones((1, 128), dtype=ml_dtypes.bfloat16),
        "ones32": np.ones((1, 128), dtype=f32),
        "mask": np.where(np.tril(np.ones((128, 128), dtype=bool)), 0.0,
                         BIGNEG).astype(f32),
    }
    in_maps = []
    for c in range(n_cores):
        m = dict(base)
        xs_ = x[c * n_seq:(c + 1) * n_seq, :T, :]
        m["x_in"] = np.ascontiguousarray(xs_.reshape(n_seq * T, D))
        in_maps.append(m)
    return in_maps


def kernel(**inputs):
    n_cores = 8
    t_tiles = 16
    n_seq = 2
    nc, _ = build_program(t_tiles=t_tiles, n_seq=n_seq)
    in_maps = _host_inputs(inputs, n_cores, t_tiles, n_seq)
    res = run_bass_kernel_spmd(nc, in_maps, core_ids=list(range(n_cores)))
    T = t_tiles * 128
    out = np.concatenate(
        [np.asarray(r["out"]).reshape(n_seq, T, D) for r in res.results], axis=0)
    return out


# version 12
# speedup vs baseline: 1.0461x; 1.0461x over previous
import sys
if "/opt/trn_rl_repo" not in sys.path:
    sys.path.insert(0, "/opt/trn_rl_repo")

from contextlib import ExitStack

import numpy as np
import ml_dtypes

import concourse.bass as bass
import concourse.mybir as mybir
from concourse.bass_utils import run_bass_kernel_spmd

F32 = mybir.dt.float32
BF16 = mybir.dt.bfloat16
AX = mybir.AxisListType.X
AF = mybir.ActivationFunctionType
ALU = mybir.AluOpType

ENGS = ("tensor", "vector", "scalar", "sync", "gpsimd")
ESEM = {"tensor": "T", "vector": "V", "scalar": "S", "sync": "Y", "gpsimd": "P"}

D = 1024
NCH = 8
HD = 64
NG = 8
DFF = 4096
NEXP = 8
NPHASE = 4
JCH = 8
NL = 2
EPS = 1e-5
BIGNEG = -1.0e30
SCALE = 0.125


class _Op:
    __slots__ = ("deps", "build", "sem", "amt", "used", "val", "eng", "seq", "cond")

    def __init__(self, eng, build, deps, sem, amt, seq, cond):
        self.eng = eng
        self.build = build
        self.deps = deps
        self.sem = sem
        self.amt = amt
        self.used = False
        self.val = None
        self.seq = seq
        self.cond = cond


class Tok:
    __slots__ = ("op",)

    def __init__(self, op):
        self.op = op


class Prog:
    def __init__(self):
        self.stream = {e: [] for e in ENGS}
        self.n_ops = 0
        self.const_tok = None
        self.cond_ctx = None

    def op(self, eng, build, deps=(), sem=None, amt=1):
        dd = []
        if eng == "scalar" and self.const_tok is not None:
            dd.append(self.const_tok)
        for d in deps:
            if d is None:
                continue
            if isinstance(d, (list, tuple)):
                for x in d:
                    if x is not None:
                        dd.append(x)
            else:
                dd.append(d)
        if eng == "tensor":
            dd = [d for d in dd if d.op.sem != "T"]
        o = _Op(eng, build, dd, sem or ESEM[eng], amt, self.n_ops, self.cond_ctx)
        self.stream[eng].append(o)
        self.n_ops += 1
        return Tok(o)

    def finalize_and_emit(self, block, sems):
        last_by = {}
        for e in ENGS:
            for o in self.stream[e]:
                if o.cond is not None:
                    last_by[(id(o.cond), o.sem)] = o
        for e in ENGS:
            for o in self.stream[e]:
                for idx, d in enumerate(o.deps):
                    dop = d.op
                    if dop.cond is not None and dop.cond is not o.cond:
                        o.deps[idx] = Tok(last_by[(id(dop.cond), dop.sem)])

        consumers = {}
        for e in ENGS:
            for o in self.stream[e]:
                for d in o.deps:
                    consumers.setdefault(id(d.op), []).append(o)
        by_sem = {}
        for e in ENGS:
            for o in self.stream[e]:
                if o.amt == 16:
                    by_sem.setdefault(o.sem, []).append(o)
        for s, dmas in by_sem.items():
            engs = {o.eng for o in dmas}
            assert len(engs) == 1, f"DMA sem {s} used from {engs}"
            for k in range(len(dmas) - 1):
                nxt = dmas[k + 1]
                ok_cond = (None, nxt.cond if nxt.cond is dmas[k].cond else None)
                cs = [c for c in consumers.get(id(dmas[k]), [])
                      if c.seq < nxt.seq and (c.cond is None or c.cond in ok_cond
                                              and c.cond is dmas[k].cond)]
                if cs:
                    nxt.deps.append(Tok(min(cs, key=lambda o: o.seq)))
                else:
                    nxt.deps.append(Tok(dmas[k]))

        for e in ENGS:
            for o in self.stream[e]:
                if o.amt == 16:
                    o.used = True
                if o.cond is not None:
                    o.cond[2].op.used = True
                for d in o.deps:
                    d.op.used = True
        cnt = {}
        for e in ENGS:
            for o in self.stream[e]:
                if o.used:
                    cnt[o.sem] = cnt.get(o.sem, 0) + o.amt
                    o.val = cnt[o.sem]
        self.sem_totals = cnt

        final_waits = {e: [] for e in ENGS}
        for s, dmas in by_sem.items():
            final_waits[dmas[0].eng].append((s, cnt[s]))

        for e in ENGS:
            ops = self.stream[e]
            if not ops and not final_waits[e]:
                continue

            def body(eng, ops=ops, e=e):
                cov = {}

                def emit_one(o):
                    ws = {}
                    for d in o.deps:
                        dop = d.op
                        if dop.val is not None and ws.get(dop.sem, 0) < dop.val:
                            ws[dop.sem] = dop.val
                    for s in sorted(ws):
                        v = ws[s]
                        if cov.get(s, 0) >= v:
                            continue
                        cov[s] = v
                        eng.wait_ge(sems[s], v)
                    inst = o.build(eng)
                    if o.used:
                        inst.then_inc(sems[o.sem], o.amt)

                creg = [None]
                i = 0
                n = len(ops)
                while i < n:
                    o = ops[i]
                    if o.cond is None:
                        emit_one(o)
                        i += 1
                        continue
                    j = i
                    while j < n and ops[j].cond is o.cond:
                        j += 1
                    grp = ops[i:j]
                    _, flag_ap, flag_tok = o.cond
                    ft = flag_tok.op
                    if ft.val is not None and cov.get(ft.sem, 0) < ft.val:
                        cov[ft.sem] = ft.val
                        eng.wait_ge(sems[ft.sem], ft.val)
                    if creg[0] is None:
                        creg[0] = eng.alloc_register(name="condflag_" + e)
                    eng.reg_load(creg[0], flag_ap)
                    deltas = {}
                    for oo in grp:
                        if oo.val is not None:
                            deltas[oo.sem] = deltas.get(oo.sem, 0) + oo.amt
                    outer_cov = dict(cov)
                    with eng.If_ne(creg[0], 0):
                        for oo in grp:
                            emit_one(oo)
                    with eng.Else():
                        for s, v in deltas.items():
                            end_v = max(oo.val for oo in grp
                                        if oo.val is not None and oo.sem == s)
                            eng.wait_ge(sems[s], end_v - v)
                            eng.sem_inc(sems[s], v)
                    eng.end_ifs()
                    cov.clear()
                    cov.update(outer_cov)
                    i = j
                for s, v in final_waits[e]:
                    if cov.get(s, 0) < v:
                        eng.wait_ge(sems[s], v)

            getattr(block, e)(body)


class Slot:

    def __init__(self):
        self.w = []
        self.rs = []

    def for_write(self):
        return list(self.w) + list(self.rs)

    def wrote(self, tok):
        self.w = [tok]
        self.rs = []

    def wrote_multi(self, toks):
        self.w = [t for t in toks if t is not None]
        self.rs = []

    def for_read(self):
        return list(self.w)

    def read(self, tok):
        self.rs.append(tok)


def build_program(t_tiles=16, n_seq=2, debug_stop=None):
    T = t_tiles * 128
    TT = t_tiles
    assert T % 512 == 0
    NGRP = T // 512

    nc = bass.Bass("TRN2", target_bir_lowering=False)

    d_x = nc.dram_tensor("x_in", [n_seq * T, D], F32, kind="ExternalInput")
    d_wqkv = nc.dram_tensor("wqkv", [D, 3 * D], F32, kind="ExternalInput")
    d_wout = nc.dram_tensor("wout", [D, D], F32, kind="ExternalInput")
    d_bout = nc.dram_tensor("bout", [1, D], F32, kind="ExternalInput")
    d_wq = nc.dram_tensor("wq", [NL * D, D], F32, kind="ExternalInput")
    d_wk = nc.dram_tensor("wk", [NL * D, D], F32, kind="ExternalInput")
    d_wv = nc.dram_tensor("wv", [NL * D, D], F32, kind="ExternalInput")
    d_wo = nc.dram_tensor("wo", [NL * D, D], F32, kind="ExternalInput")
    d_lnstack = nc.dram_tensor("lnstack", [NL * 32, 128], F32, kind="ExternalInput")
    d_bqk = nc.dram_tensor("bqk", [NL * 16, 128], F32, kind="ExternalInput")
    d_bv = nc.dram_tensor("bv", [NL, D], F32, kind="ExternalInput")
    d_bo = nc.dram_tensor("bo", [NL, D], F32, kind="ExternalInput")
    d_gw = nc.dram_tensor("gatew", [NL * D, NEXP], F32, kind="ExternalInput")
    d_gb = nc.dram_tensor("gateb", [1, NL * NEXP], F32, kind="ExternalInput")
    d_ew1 = nc.dram_tensor("ew1", [NL * NEXP * D, DFF], F32, kind="ExternalInput")
    d_eb1 = nc.dram_tensor("eb1", [NL * NEXP * 32, 128], F32, kind="ExternalInput")
    d_ew2 = nc.dram_tensor("ew2", [NL * NEXP * DFF, D], F32, kind="ExternalInput")
    d_eb2 = nc.dram_tensor("eb2", [NL * NEXP, D], F32, kind="ExternalInput")
    d_eyeb = nc.dram_tensor("eyeb", [128, 128], BF16, kind="ExternalInput")
    d_eye32 = nc.dram_tensor("eye32", [128, 128], F32, kind="ExternalInput")
    d_onesb = nc.dram_tensor("onesb", [1, 128], BF16, kind="ExternalInput")
    d_ones32 = nc.dram_tensor("ones32", [1, 128], F32, kind="ExternalInput")
    d_mask = nc.dram_tensor("mask", [128, 128], F32, kind="ExternalInput")
    d_out = nc.dram_tensor("out", [n_seq * T, D], F32, kind="ExternalOutput")
    d_xsave = nc.dram_tensor("xsave", [T, D], F32)

    es = ExitStack()
    sbt = lambda name, shape, dt: es.enter_context(nc.sbuf_tensor(name, shape, dt))
    pst_ = lambda name, shape, dt: es.enter_context(nc.psum_tensor(name, shape, dt))

    A = sbt("A", [128, TT * 1024], F32)
    HF = sbt("HF", [128, NCH * T], BF16)
    CB = sbt("CB", [128, NCH * T], BF16)
    QB = sbt("QB", [128, T], BF16)
    KBf = sbt("KB", [128, T], BF16)
    VB = sbt("VB", [128, T], BF16)
    PB = sbt("PB", [128, T], BF16)
    PTB = sbt("PTB", [128, T], BF16)
    DUM = sbt("DUM", [128, 1024], BF16)
    NT = sbt("NT", [128, 2048], BF16)
    NT32 = sbt("NT32", [128, 2048], F32)
    H32 = sbt("H32", [128, 256], F32)
    WS = sbt("WS", [128, 2 * 1024], F32)
    WB = sbt("WB", [128, 3 * 1024], BF16)
    WB2 = sbt("WB2", [128, 8 * 1024], BF16)
    ST = sbt("STAT", [128, 64], F32)
    GS = sbt("GS", [128, 80], F32)
    CW = sbt("CW", [128, TT * 8], F32)
    CWS = sbt("CWS", [128, 8], F32)
    MEAN = sbt("MEAN", [128, 16], F32)
    B1C = sbt("B1C", [128, 32], F32)
    LNC = sbt("LNC", [128, 32], F32)
    BQC = sbt("BQC", [128, 16], F32)
    BRT = sbt("BRT", [32, 128], F32)
    BR32 = sbt("BR32", [1, 256], F32)
    BRB = sbt("BRB", [1, 3072], BF16)
    GB = sbt("GBrow", [1, NL * NEXP], F32)
    GW = sbt("GW", [128, NL * 64], F32)
    FLAGS = sbt("FLAGS", [1, 8], mybir.dt.int32)
    EYEB = sbt("EYEB", [128, 128], BF16)
    EYE32 = sbt("EYE32", [128, 128], F32)
    ONESB = sbt("ONESB", [1, 128], BF16)
    ONES32 = sbt("ONES32", [1, 128], F32)
    MSK = sbt("MSK", [128, 128], F32)

    PS_S = pst_("PS_S", [128, 2048], F32)
    PS_M0 = pst_("PS_M0", [128, 512], F32)
    PS_M1 = pst_("PS_M1", [128, 512], F32)
    PS_T0 = pst_("PS_T0", [128, 1024], BF16)
    PS_T1 = pst_("PS_T1", [128, 1024], BF16)
    PSM = [PS_M0, PS_M1]
    PST = [PS_T0, PS_T1]

    sems = {}
    for name in ("T", "V", "S", "Y", "P",
                 "dS0", "dS1", "dS2", "dS3", "dG0", "dG1", "dG2", "dG3"):
        sems[name] = es.enter_context(nc.semaphore(name))

    p = Prog()

    bS = [Slot() for _ in range(4)]
    bM = [Slot(), Slot()]
    bT = [Slot(), Slot()]

    a_w = [None] * TT
    a_r = [[] for _ in range(TT)]
    hf = Slot()
    cb = Slot()
    qb = Slot()
    kb = Slot()
    vb = Slot()
    pbh = [Slot(), Slot()]
    ptbh = [Slot(), Slot()]
    dum_s = Slot()
    nt = [Slot(), Slot()]
    nt32 = [Slot(), Slot()]
    h32 = [Slot(), Slot()]
    st_slots = {}
    cw_s = Slot()
    cws_s = Slot()
    mean_s = Slot()
    b1c_s = Slot()
    lnc_s = Slot()
    bqc_s = Slot()
    brt_s = Slot()
    br32_s = Slot()
    brb = [Slot(), Slot(), Slot(), Slot()]
    gw_s = Slot()
    gb_s = Slot()
    seq_gate_tok = [None]
    out_toks = []

    def stslot(kind, par):
        key = (kind, par)
        if key not in st_slots:
            st_slots[key] = Slot()
        return st_slots[key]

    def Asl(t):
        return A[:, t * 1024:(t + 1) * 1024]

    dma_rot = {"sync": [0], "gpsimd": [0]}

    def dma(queue, dst, src, deps=()):
        r = dma_rot[queue]
        sem = ("dS" if queue == "sync" else "dG") + str(r[0] % 4)
        r[0] += 1
        return p.op(queue, lambda e, d=dst, s=src: e.dma_start(out=d, in_=s),
                    deps=deps, sem=sem, amt=16)

    ws_cast = [None] * 2
    ws_i = [0]

    def stage(src_ap, dst_ap, dst_deps):
        s = ws_i[0] % 2
        ws_i[0] += 1
        st_ap = WS[:, s * 1024:(s + 1) * 1024]
        if len(src_ap.shape) == 3:
            st_view = st_ap.rearrange("p (c n) -> p c n", n=src_ap.shape[2])
        else:
            st_view = st_ap
        d = dma("sync", st_view, src_ap, deps=[ws_cast[s]])
        c = p.op("vector", lambda e, o=dst_ap, i_=st_ap: e.tensor_copy(o, i_),
                 deps=[d] + dst_deps)
        ws_cast[s] = c
        return c

    wb_users = [[] for _ in range(3)]
    wb_cast = [None] * 3
    wb_i = [0]

    def wload(src_ap):
        s = wb_i[0] % 3
        wb_i[0] += 1
        ap_ = WB[:, s * 1024:(s + 1) * 1024]
        ctok = stage(src_ap, ap_, wb_users[s] + [wb_cast[s]])
        wb_users[s] = []
        wb_cast[s] = ctok
        return s, ap_, ctok

    def wuse(s, tok):
        wb_users[s].append(tok)

    wb2_users = [[] for _ in range(NCH)]
    wb2_cast = [None] * NCH

    def w2load(j, src_ap):
        ap_ = WB2[:, j * 1024:(j + 1) * 1024]
        ctok = stage(src_ap, ap_, wb2_users[j] + [wb2_cast[j]])
        wb2_users[j] = []
        wb2_cast[j] = ctok
        return ap_, ctok

    def colslice(dram_cols):
        return dram_cols.rearrange("(c p) n -> p c n", p=128)


    def load_row_bf16(src_row, dst_off, slot):
        tok = None
        for hh_ in range(4):
            t1_ = dma("sync", BR32[:, :], src_row[:, hh_ * 256:(hh_ + 1) * 256],
                      deps=br32_s.for_write())
            br32_s.wrote(t1_)
            t2_ = p.op("vector",
                       lambda e, o=BRB[0:1, dst_off + hh_ * 256: dst_off + (hh_ + 1) * 256]:
                       e.tensor_copy(o, BR32[:, :]),
                       deps=[t1_] + (slot.for_write() if hh_ == 0 else []))
            br32_s.read(t2_)
            tok = t2_
        slot.wrote(tok)
        return tok

    CONSTS = sbt("CONSTS", [128, 2], F32)
    c0 = p.op("vector", lambda e: e.memset(CONSTS[:, 0:1], 0.0))
    c1 = p.op("vector", lambda e: e.memset(CONSTS[:, 1:2], EPS), deps=[c0])
    nc.const_aps.register(F32, 0.0, CONSTS[:, 0:1])
    nc.const_aps.register(F32, EPS, CONSTS[:, 1:2])
    p.const_tok = c1

    eyeb_tok = dma("sync", EYEB[:, :], d_eyeb.ap())
    eye32_tok = dma("sync", EYE32[:, :], d_eye32.ap())
    onesb_tok = dma("sync", ONESB[:, :], d_onesb.ap())
    ones32_tok = dma("sync", ONES32[:, :], d_ones32.ap())
    msk_tok = dma("sync", MSK[:, :], d_mask.ap())
    gb_s.wrote(dma("sync", GB[:, :], d_gb.ap()))
    gw_tok = None
    for l in range(NL):
        gw_tok = dma("sync",
                     GW[:, l * 64:(l + 1) * 64].rearrange("p (c n) -> p c n", n=NEXP),
                     d_gw.ap()[l * D:(l + 1) * D, :].rearrange("(c p) n -> p c n", p=128))
    gw_s.wrote(gw_tok)

    def ln_stats(t, src_ap, deps, par, banks=()):
        b = par * 8
        sl = stslot("ln", par)
        sdep = list(deps) + sl.for_write()
        s1 = ST[:, b:b + 1]
        s2 = ST[:, b + 1:b + 2]
        m = ST[:, b + 2:b + 3]
        nm2 = ST[:, b + 3:b + 4]
        var = ST[:, b + 4:b + 5]
        lnv = ST[:, b + 5:b + 6]
        rstd = ST[:, b + 6:b + 7]
        t_s1 = p.op("vector", lambda e: e.reduce_sum(s1, src_ap, AX), deps=sdep)
        t_s2 = p.op("scalar", lambda e: e.activation(DUM[:, :], src_ap, AF.Square,
                                                     accum_out=s2),
                    deps=sdep + dum_s.for_write() + ([t_s1] if banks else []))
        dum_s.wrote(t_s2)
        for bk in banks:
            bk.read(t_s1)
            bk.read(t_s2)
        t_m = p.op("vector", lambda e: e.tensor_scalar(m, s1, 1.0 / D, None, ALU.mult),
                   deps=[t_s1])
        t_nm2 = p.op("vector", lambda e: e.tensor_scalar(nm2, m, m, -1.0, ALU.mult, ALU.mult),
                     deps=[t_m])
        t_var = p.op("vector", lambda e: e.scalar_tensor_tensor(var, s2, 1.0 / D, nm2,
                                                                ALU.mult, ALU.add),
                     deps=[t_s2, t_nm2])
        t_ln = p.op("scalar", lambda e: e.activation(lnv, var, AF.Ln, bias=EPS, scale=1.0),
                    deps=[t_var])
        t_r = p.op("scalar", lambda e: e.activation(rstd, lnv, AF.Exp, scale=-0.5),
                   deps=[t_ln])
        sl.wrote(t_r)
        return m, rstd, t_r

    def emit_ln_to_hf(t, with_lnc):
        par = t % 2
        m, rstd, t_r = ln_stats(t, Asl(t), [a_w[t]], par)
        ntsl = nt[par]
        ntap = NT[:, par * 1024:(par + 1) * 1024]
        t_ap = p.op("vector",
                    lambda e, t=t, ntap=ntap, m=m, rstd=rstd: e.tensor_scalar(
                        ntap, Asl(t), m, rstd, ALU.subtract, ALU.mult),
                    deps=[a_w[t], t_r] + ntsl.for_write())
        a_r[t].append(t_ap)
        ntsl.wrote(t_ap)
        stslot("ln", par).read(t_ap)
        if with_lnc:
            wdeps2 = hf.for_write()
            lt = [None, None]
            for c in range(NCH):
                bk = c // 4
                pslice = PST[bk][:, (c % 4) * 128:(c % 4 + 1) * 128]
                lt[bk] = p.op("tensor",
                              lambda e, o=pslice, s=ntap[:, c * 128:(c + 1) * 128]:
                              e.transpose(o, s, EYEB[:, :]),
                              deps=[t_ap, eyeb_tok]
                              + (bT[bk].for_write() if c % 4 == 0 else []))
            bT[0].wrote(lt[0])
            bT[1].wrote(lt[1])
            et_sc = et_v = None
            for c in range(NCH):
                bk = c // 4
                pslice = PST[bk][:, (c % 4) * 128:(c % 4 + 1) * 128]
                hfdst = HF[:, c * T + t * 128: c * T + t * 128 + 128]
                if bk == 0:
                    et_sc = p.op("scalar",
                                 lambda e, o=hfdst, i_=pslice, c=c: e.activation(
                                     o, i_, AF.Identity,
                                     bias=LNC[:, 8 + c:9 + c], scale=LNC[:, c:c + 1]),
                                 deps=[lt[0], lnc_s.for_read()] + wdeps2)
                    lnc_s.read(et_sc)
                    bT[0].read(et_sc)
                else:
                    et_v = p.op("vector",
                                lambda e, o=hfdst, i_=pslice, c=c: e.tensor_scalar(
                                    o, i_, LNC[:, c:c + 1], LNC[:, 8 + c:9 + c],
                                    ALU.mult, ALU.add),
                                deps=[lt[1], lnc_s.for_read()] + wdeps2)
                    lnc_s.read(et_v)
                    bT[1].read(et_v)
            hf.wrote_multi([et_sc, et_v])
            et = et_sc
        else:
            bank = bT[t % 2]
            wdeps = bank.for_write()
            last_tt = None
            for c in range(NCH):
                pslice = PST[t % 2][:, c * 128:(c + 1) * 128]
                last_tt = p.op("tensor",
                               lambda e, o=pslice, s=ntap[:, c * 128:(c + 1) * 128]:
                               e.transpose(o, s, EYEB[:, :]),
                               deps=[t_ap, eyeb_tok] + (wdeps if c == 0 else []))
            bank.wrote(last_tt)
            hfv = HF[:, :].rearrange("p (c n) -> p c n", c=NCH)[:, :, t * 128: t * 128 + 128]
            psv = PST[t % 2][:, 0:NCH * 128].rearrange("p (c n) -> p c n", c=NCH)
            et = p.op("scalar",
                      lambda e, o=hfv, i_=psv: e.activation(o, i_, AF.Identity),
                      deps=[last_tt] + hf.for_write())
            bank.read(et)
            hf.wrote(et)
        ntsl.read(et)

    def emit_attention(l, outer):
        attn_idx = [0]
        attn_pend = []

        def emit_trans_av(ctx):
            hh, qt, g2, t_nm = ctx["hh"], ctx["qt"], ctx["g"], ctx["t_nm"]
            reg = ctx["reg"]
            regs = ctx["regs"]
            nkt = qt + 1
            GS4 = 4
            ngrp4 = (nkt + GS4 - 1) // GS4
            ev_toks = []
            for g4 in range(ngrp4):
                k0 = g4 * GS4
                cnt = min(GS4, nkt - k0)
                bank = bT[g4 % 2]
                slot0 = ((g4 // 2) % 2) * GS4
                wdeps = bank.for_write()
                last_tt = None
                for kt in range(k0, k0 + cnt):
                    pslice = PST[g4 % 2][:, (slot0 + kt - k0) * 128:(slot0 + kt - k0 + 1) * 128]
                    last_tt = p.op("tensor",
                                   lambda e, o=pslice, kt=kt: e.transpose(
                                       o, PB[:, reg + kt * 128: reg + (kt + 1) * 128],
                                       EYEB[:, :]),
                                   deps=[t_nm, eyeb_tok] + (wdeps if kt == k0 else []))
                    for r in regs:
                        pbh[r].read(last_tt)
                bank.wrote(last_tt)
                ptw = []
                for r in regs:
                    ptw += ptbh[r].for_write()
                et = p.op("scalar",
                          lambda e, o=PTB[:, reg + k0 * 128: reg + (k0 + cnt) * 128],
                          i_=PST[g4 % 2][:, slot0 * 128:(slot0 + cnt) * 128]:
                          (e.copy(o, i_) if hasattr(e, "copy") else e.tensor_copy(o, i_)),
                          deps=[last_tt] + ptw)
                bank.read(et)
                for r in regs:
                    ptbh[r].wrote(et)
                ev_toks.append(et)
            ai = attn_idx[0]
            avbank = bM[ai % 2]
            avdeps = [vb.for_read()] + avbank.for_write()
            last = None
            for kt in range(nkt):
                last = p.op("tensor",
                            lambda e, kt=kt, hh=hh, qt=qt, ai=ai, st=(kt == 0):
                            e.matmul(PSM[ai % 2][hh * 64:(hh + 1) * 64, 0:128],
                                     VB[:, kt * 128 + hh * 64: kt * 128 + (hh + 1) * 64],
                                     PTB[:, reg + kt * 128: reg + (kt + 1) * 128],
                                     start=st, stop=(kt == qt)),
                            deps=([ev_toks[kt // GS4]] + (avdeps if kt == 0 else [])))
            avbank.wrote(last)
            for r in regs:
                ptbh[r].read(last)
            vb.read(last)
            yt = p.op("scalar",
                      lambda e, hh=hh, qt=qt, g2=g2, ai=ai:
                      e.activation(CB[hh * 64:(hh + 1) * 64,
                                      g2 * T + qt * 128: g2 * T + qt * 128 + 128],
                                   PSM[ai % 2][hh * 64:(hh + 1) * 64, 0:128],
                                   AF.Identity),
                      deps=[last] + cb.for_write())
            avbank.read(yt)
            cb.wrote(yt)

        for g in range(NG):
            if outer:
                sq, wq_ap, wq_tok = wload(colslice(d_wqkv.ap()[:, g * 128:(g + 1) * 128]))
                sk, wk_ap, wk_tok = wload(colslice(d_wqkv.ap()[:, D + g * 128: D + (g + 1) * 128]))
                sv, wv_ap, wv_tok = wload(colslice(d_wqkv.ap()[:, 2 * D + g * 128: 2 * D + (g + 1) * 128]))
                wo_ap, wo_tok = w2load(g, d_wout.ap()[g * 128:(g + 1) * 128, :])
            else:
                sq, wq_ap, wq_tok = wload(colslice(d_wq.ap()[l * D:(l + 1) * D, :][:, g * 128:(g + 1) * 128]))
                sk, wk_ap, wk_tok = wload(colslice(d_wk.ap()[l * D:(l + 1) * D, :][:, g * 128:(g + 1) * 128]))
                sv, wv_ap, wv_tok = wload(colslice(d_wv.ap()[l * D:(l + 1) * D, :][:, g * 128:(g + 1) * 128]))
                wo_ap, wo_tok = w2load(g, d_wo.ap()[l * D + g * 128: l * D + (g + 1) * 128, :])
            if g == 0:
                wo_all = []
            wo_all.append((g, wo_ap, wo_tok))

            for w_s, w_ap, w_tok, dst, dsts, bias_col in (
                (sq, wq_ap, wq_tok, QB, qb, (None if outer else BQC[:, g:g + 1])),
                (sk, wk_ap, wk_tok, KBf, kb, (None if outer else BQC[:, 8 + g:9 + g])),
            ):
                for grp in range(NGRP):
                    bank = bM[grp % 2]
                    mmdeps = [w_tok, hf.for_read()] + bank.for_write()
                    last = None
                    for c in range(NCH):
                        last = p.op("tensor",
                                    lambda e, c=c, w=w_ap, gp=grp, st=(c == 0):
                                    e.matmul(PSM[gp % 2][:, :],
                                             w[:, c * 128:(c + 1) * 128],
                                             HF[:, c * T + gp * 512: c * T + gp * 512 + 512],
                                             start=st, stop=(c == NCH - 1)),
                                    deps=(mmdeps if c == 0 else [w_tok]))
                    bank.wrote(last)
                    hf.read(last)
                    wuse(w_s, last)
                    sc_ = SCALE if dst is QB else 1.0
                    if bias_col is None:
                        et = p.op("scalar",
                                  lambda e, o=dst[:, grp * 512:(grp + 1) * 512],
                                  i_=PSM[grp % 2][:, :], sc_=sc_:
                                  e.activation(o, i_, AF.Identity, scale=sc_),
                                  deps=[last] + dsts.for_write())
                    else:
                        et = p.op("scalar",
                                  lambda e, o=dst[:, grp * 512:(grp + 1) * 512],
                                  i_=PSM[grp % 2][:, :], bi=bias_col, sc_=sc_:
                                  e.activation(o, i_, AF.Identity, bias=bi, scale=sc_),
                                  deps=[last, bqc_s.for_read()] + dsts.for_write())
                        bqc_s.read(et)
                    bank.read(et)
                    dsts.wrote(et)

            for t in range(TT):
                bank = bM[t % 2]
                mmdeps = [wv_tok, hf.for_read()] + bank.for_write()
                last = None
                for c in range(NCH):
                    last = p.op("tensor",
                                lambda e, c=c, t=t, w=wv_ap, st=(c == 0),
                                fin=(c == NCH - 1 and outer):
                                e.matmul(PSM[t % 2][:, 0:128],
                                         HF[:, c * T + t * 128: c * T + t * 128 + 128],
                                         w[:, c * 128:(c + 1) * 128],
                                         start=st, stop=fin),
                                deps=(mmdeps if c == 0 else [wv_tok]))
                hf.read(last)
                if not outer:
                    last = p.op("tensor",
                                lambda e, t=t, g=g: e.matmul(
                                    PSM[t % 2][:, 0:128],
                                    ONESB[0:1, 0:128],
                                    BRB[0:1, 2048 + g * 128: 2048 + (g + 1) * 128],
                                    start=False, stop=True),
                                deps=[onesb_tok, brb[3].for_read()])
                    brb[3].read(last)
                bank.wrote(last)
                wuse(sv, last)
                et = p.op("scalar",
                          lambda e, t=t: e.activation(VB[:, t * 128:(t + 1) * 128],
                                                      PSM[t % 2][:, 0:128], AF.Identity),
                          deps=[last] + vb.for_write())
                bank.read(et)
                vb.wrote(et)

            for hh in range(2):
                for qt in range(TT):
                    klen = (qt + 1) * 128
                    n512 = (klen + 511) // 512
                    short = klen <= min(1024, T // 2)
                    base = 2 * (attn_idx[0] % 2) if short else 0
                    off = base * 512
                    par = attn_idx[0] % 2
                    b = 16 + par * 8
                    asl = stslot("att", par)
                    mx = ST[:, b:b + 1]
                    ngm = ST[:, b + 1:b + 2]
                    sm = ST[:, b + 2:b + 3]
                    rc = ST[:, b + 3:b + 4]
                    cmx = ST[:, b + 4:b + 8]
                    bank_toks = []
                    mx_toks = []
                    sdeps = [qb.for_read(), kb.for_read()] + asl.for_write()
                    for c5 in range(n512):
                        nn = min(512, klen - c5 * 512)
                        bk = bS[base + c5]
                        mt = p.op("tensor",
                                  lambda e, c5=c5, nn=nn, hh=hh, qt=qt, off=off:
                                  e.matmul(PS_S[:, off + c5 * 512: off + c5 * 512 + nn],
                                           QB[hh * 64:(hh + 1) * 64, qt * 128:(qt + 1) * 128],
                                           KBf[hh * 64:(hh + 1) * 64, c5 * 512: c5 * 512 + nn],
                                           start=True, stop=True),
                                  deps=sdeps + bk.for_write())
                        bk.wrote(mt)
                        bank_toks.append(mt)
                        if c5 == n512 - 1:
                            mk = p.op("vector",
                                      lambda e, qt=qt, off=off: e.tensor_tensor(
                                          PS_S[:, off + qt * 128: off + (qt + 1) * 128],
                                          PS_S[:, off + qt * 128: off + (qt + 1) * 128],
                                          MSK[:, :], ALU.add),
                                      deps=[msk_tok, mt] + bk.for_write())
                            bk.wrote(mk)
                            tmx = p.op("vector",
                                       lambda e, c5=c5, nn=nn, off=off: e.reduce_max(
                                           cmx[:, c5:c5 + 1],
                                           PS_S[:, off + c5 * 512: off + c5 * 512 + nn], AX),
                                       deps=[mk])
                        else:
                            tmx = p.op("vector",
                                       lambda e, c5=c5, nn=nn, off=off: e.reduce_max(
                                           cmx[:, c5:c5 + 1],
                                           PS_S[:, off + c5 * 512: off + c5 * 512 + nn], AX),
                                       deps=[mt])
                        bk.read(tmx)
                        mx_toks.append(tmx)
                    qb.read(bank_toks[-1])
                    kb.read(bank_toks[-1])

                    regs = ((attn_idx[0] % 2),) if short else (0, 1)
                    reg = (attn_idx[0] % 2) * (T // 2) if short else 0
                    keep = []
                    for pd in attn_pend:
                        if set(pd["regs"]) & set(regs):
                            emit_trans_av(pd)
                        else:
                            keep.append(pd)
                    attn_pend[:] = keep

                    attn_idx[0] += 1
                    t_mx = p.op("vector",
                                lambda e, n512=n512, mx=mx: e.reduce_max(
                                    mx, cmx[:, 0:n512], AX),
                                deps=mx_toks)
                    t_ng = p.op("vector",
                                lambda e, mx=mx, ngm=ngm: e.tensor_scalar(
                                    ngm, mx, -1.0, None, ALU.mult),
                                deps=[t_mx])
                    pbw = []
                    for r in regs:
                        pbw += pbh[r].for_write()
                    t_ex = p.op("scalar",
                                lambda e, klen=klen, off=off, reg=reg, ngm=ngm, sm=sm:
                                e.activation(
                                    PB[:, reg:reg + klen], PS_S[:, off:off + klen],
                                    AF.Exp, bias=ngm, scale=1.0, accum_out=sm),
                                deps=[t_ng] + bank_toks + pbw)
                    for c5 in range(n512):
                        bS[base + c5].read(t_ex)
                    for r in regs:
                        pbh[r].wrote(t_ex)
                    t_rc = p.op("vector", lambda e, rc=rc, sm=sm: e.reciprocal(rc, sm),
                                deps=[t_ex])
                    t_nm = p.op("vector",
                                lambda e, klen=klen, reg=reg, rc=rc: e.tensor_scalar(
                                    PB[:, reg:reg + klen], PB[:, reg:reg + klen],
                                    rc, None, ALU.mult),
                                deps=[t_rc, t_ex])
                    for r in regs:
                        pbh[r].wrote(t_nm)
                    asl.wrote(t_nm)
                    attn_pend.append({"hh": hh, "qt": qt, "g": g, "t_nm": t_nm,
                                      "reg": reg, "regs": regs})

            for pd in attn_pend:
                emit_trans_av(pd)
            attn_pend[:] = []

        w2cast_last = wo_all[-1][2]
        for t in range(TT):
            half = t % 2
            off = half * 1024
            banks = [bS[half * 2], bS[half * 2 + 1]]
            mmdeps = ([cb.for_read(), w2cast_last]
                      + banks[0].for_write() + banks[1].for_write())
            last = None
            for g, wo_ap, wo_tok in wo_all:
                for piece in range(2):
                    last = p.op("tensor",
                                lambda e, g=g, piece=piece, t=t, off=off, wo_ap=wo_ap,
                                st=(g == 0 and piece == 0):
                                e.matmul(PS_S[:, off + piece * 512: off + piece * 512 + 512],
                                         CB[:, g * T + t * 128: g * T + t * 128 + 128],
                                         wo_ap[:, piece * 512:(piece + 1) * 512],
                                         start=(g == 0), stop=False),
                                deps=(mmdeps if (g == 0 and piece == 0) else []))
            cb.read(last)
            rowslot = brb[1]
            rowoff = 1024
            for piece in range(2):
                last = p.op("tensor",
                            lambda e, piece=piece, off=off, rowoff=rowoff:
                            e.matmul(PS_S[:, off + piece * 512: off + piece * 512 + 512],
                                     ONESB[0:1, 0:128],
                                     BRB[0:1, rowoff + piece * 512: rowoff + (piece + 1) * 512],
                                     start=False, stop=True),
                            deps=[onesb_tok, rowslot.for_read()])
            rowslot.read(last)
            banks[0].wrote(last)
            banks[1].wrote(last)
            psap = PS_S[:, off:off + 1024]
            if outer:
                par = t % 2
                m, rstd, t_r = ln_stats(t, psap, [last], par, banks=banks)
                tmp = NT32[:, par * 1024:(par + 1) * 1024]
                t_ap2 = p.op("vector",
                             lambda e, tmp=tmp, psap=psap, m=m, rstd=rstd:
                             e.tensor_scalar(tmp, psap, m, rstd, ALU.subtract, ALU.mult),
                             deps=[t_r, last] + nt32[par].for_write())
                banks[0].read(t_ap2)
                banks[1].read(t_ap2)
                nt32[par].wrote(t_ap2)
                stslot("ln", par).read(t_ap2)
                t_add = p.op("vector",
                             lambda e, t=t, tmp=tmp: e.tensor_tensor(
                                 Asl(t), Asl(t), tmp, ALU.add),
                             deps=[t_ap2, a_w[t]] + a_r[t])
                nt32[par].read(t_add)
                a_w[t] = t_add
                a_r[t] = []
            else:
                t_add = p.op("vector",
                             lambda e, t=t, psap=psap: e.scalar_tensor_tensor(
                                 Asl(t), psap, 1.0, Asl(t), ALU.mult, ALU.add),
                             deps=[last, a_w[t]] + a_r[t])
                banks[0].read(t_add)
                banks[1].read(t_add)
                a_w[t] = t_add
                a_r[t] = []
        for g, wo_ap, wo_tok in wo_all:
            wb2_users[g].append(last)

    def emit_topk(logits_ap, ldeps, cw_ap, cw_slot, par, nrows):
        b = 32 + par * 16
        gb_ = par * 32
        sl = stslot("gate", par)
        gsl = stslot("gs", par)
        m1 = ST[0:nrows, b:b + 1]
        nm1 = ST[0:nrows, b + 1:b + 2]
        m2 = ST[0:nrows, b + 2:b + 3]
        e2c = ST[0:nrows, b + 3:b + 4]
        den = ST[0:nrows, b + 4:b + 5]
        rr = ST[0:nrows, b + 5:b + 6]
        eq1 = GS[0:nrows, gb_:gb_ + 8]
        l2 = GS[0:nrows, gb_ + 8:gb_ + 16]
        eq2 = GS[0:nrows, gb_ + 16:gb_ + 24]
        tt1 = GS[0:nrows, gb_ + 24:gb_ + 32]
        d0 = list(ldeps) + sl.for_write() + gsl.for_write()
        t1 = p.op("vector", lambda e: e.reduce_max(m1, logits_ap, AX), deps=d0)
        t2 = p.op("vector", lambda e: e.tensor_scalar(eq1, logits_ap, m1, None,
                                                      ALU.is_equal),
                  deps=[t1] + list(ldeps))
        t3 = p.op("vector", lambda e: e.scalar_tensor_tensor(l2, eq1, BIGNEG, logits_ap,
                                                             ALU.mult, ALU.add),
                  deps=[t2] + list(ldeps))
        t4 = p.op("vector", lambda e: e.reduce_max(m2, l2, AX), deps=[t3])
        t5 = p.op("vector", lambda e: e.tensor_scalar(nm1, m1, -1.0, None, ALU.mult),
                  deps=[t1])
        t6 = p.op("scalar", lambda e: e.activation(e2c, m2, AF.Exp, bias=nm1),
                  deps=[t4, t5])
        t7 = p.op("vector", lambda e: e.tensor_scalar(den, e2c, 1.0, None, ALU.add),
                  deps=[t6])
        t8 = p.op("vector", lambda e: e.reciprocal(rr, den), deps=[t7])
        t9 = p.op("vector", lambda e: e.tensor_scalar(eq2, l2, m2, None, ALU.is_equal),
                  deps=[t4, t3])
        t10 = p.op("vector", lambda e: e.scalar_tensor_tensor(tt1, eq2, e2c, eq1,
                                                              ALU.mult, ALU.add),
                   deps=[t9, t6, t2])
        t11 = p.op("vector", lambda e: e.tensor_scalar(cw_ap, tt1, rr, None, ALU.mult),
                   deps=[t10, t8] + cw_slot.for_write())
        cw_slot.wrote(t11)
        sl.wrote(t11)
        gsl.wrote(t11)
        return t11

    def emit_ln2_gate(l):
        if l == 1:
            tz = p.op("vector", lambda e: e.memset(MEAN[:, 0:9], 0.0),
                      deps=mean_s.for_write())
            mean_s.wrote(tz)
        for t in range(TT):
            par = t % 2
            m, rstd, t_r = ln_stats(t, Asl(t), [a_w[t]], par)
            ntap = NT32[:, par * 1024:(par + 1) * 1024]
            t_ap = p.op("vector",
                        lambda e, t=t, ntap=ntap, m=m, rstd=rstd: e.tensor_scalar(
                            ntap, Asl(t), m, rstd, ALU.subtract, ALU.mult),
                        deps=[a_w[t], t_r] + nt32[par].for_write())
            a_r[t].append(t_ap)
            nt32[par].wrote(t_ap)
            stslot("ln", par).read(t_ap)
            gate_last = None
            for c in range(NCH):
                i = t * NCH + c
                bank = bM[i % 2]
                pslice = PSM[i % 2][:, 0:128]
                tt = p.op("tensor",
                          lambda e, o=pslice, ntap=ntap, c=c: e.transpose(
                              o, ntap[:, c * 128:(c + 1) * 128], EYE32[:, :]),
                          deps=[t_ap, eye32_tok] + bank.for_write())
                bank.wrote(tt)
                nt32[par].read(tt)
                e1 = p.op("scalar",
                          lambda e, o=HF[:, c * T + t * 128: c * T + t * 128 + 128],
                          i_=pslice, c=c: e.activation(o, i_, AF.Identity,
                                                       bias=LNC[:, 24 + c:25 + c],
                                                       scale=LNC[:, 16 + c:17 + c]),
                          deps=[tt, lnc_s.for_read()] + hf.for_write())
                hf.wrote(e1)
                lnc_s.read(e1)
                bank.read(e1)
                hsl = h32[i % 2]
                e2 = p.op("vector",
                          lambda e, i_=pslice, c=c, sl=(i % 2): e.tensor_scalar(
                              H32[:, sl * 128:(sl + 1) * 128], i_,
                              LNC[:, 16 + c:17 + c], LNC[:, 24 + c:25 + c],
                              ALU.mult, ALU.add),
                          deps=[tt, e1, lnc_s.for_read()] + hsl.for_write())
                lnc_s.read(e2)
                bank.read(e2)
                hsl.wrote(e2)
                if l == 0:
                    gm = p.op("tensor",
                              lambda e, t=t, c=c, sl=(i % 2), l=l, st=(c == 0):
                              e.matmul(PS_S[:, t * 8:(t + 1) * 8],
                                       H32[:, sl * 128:(sl + 1) * 128],
                                       GW[:, l * 64 + c * 8: l * 64 + (c + 1) * 8],
                                       start=st, stop=False),
                              deps=[e2, gw_s.for_read()]
                              + (bS[0].for_write() if c == 0 else []))
                    hsl.read(gm)
                    gw_s.read(gm)
                    gate_last = gm
                else:
                    tcol = MEAN[:, 8:9]
                    tr_ = p.op("vector",
                               lambda e, sl=(i % 2), tcol=tcol: e.reduce_sum(
                                   tcol, H32[:, sl * 128:(sl + 1) * 128], AX),
                               deps=[e2] + mean_s.for_write())
                    mean_s.wrote(tr_)
                    hsl.read(tr_)
                    ta_ = p.op("vector",
                               lambda e, c=c, tcol=tcol: e.tensor_tensor(
                                   MEAN[:, c:c + 1], MEAN[:, c:c + 1], tcol, ALU.add),
                               deps=[tr_] + mean_s.for_write())
                    mean_s.wrote(ta_)
            if l == 0:
                gl = p.op("tensor",
                          lambda e, t=t, l=l: e.matmul(
                              PS_S[:, t * 8:(t + 1) * 8], ONES32[0:1, 0:128],
                              GB[0:1, l * 8:(l + 1) * 8],
                              start=False, stop=True),
                          deps=[ones32_tok, gb_s.for_read(), gate_last])
                gb_s.read(gl)
                bS[0].wrote(gl)
                lg = GS[:, 64 + (t % 2) * 8: 64 + (t % 2) * 8 + 8]
                lgsl = stslot("lg", t % 2)
                t_cp = p.op("vector",
                            lambda e, t=t, lg=lg: e.tensor_copy(
                                lg, PS_S[:, t * 8:(t + 1) * 8]),
                            deps=[gl] + lgsl.for_write())
                bS[0].read(t_cp)
                lgsl.wrote(t_cp)
                tk = emit_topk(lg, [t_cp], CW[:, t * 8:(t + 1) * 8], cw_s, t % 2,
                               nrows=128)
                lgsl.read(tk)

    def emit_seq_gate(l):
        tsc = p.op("vector",
                   lambda e: e.tensor_scalar(MEAN[:, 0:8], MEAN[:, 0:8], 1.0 / T, None,
                                             ALU.mult),
                   deps=mean_s.for_write())
        mean_s.wrote(tsc)
        bank = bS[0]
        last = None
        for c in range(NCH):
            last = p.op("tensor",
                        lambda e, c=c, l=l, st=(c == 0): e.matmul(
                            PS_S[0:1, 0:8], MEAN[:, c:c + 1],
                            GW[:, l * 64 + c * 8: l * 64 + (c + 1) * 8],
                            start=st, stop=False),
                        deps=[tsc, gw_s.for_read()] + (bank.for_write() if c == 0 else []))
            gw_s.read(last)
        mean_s.read(last)
        gl = p.op("tensor",
                  lambda e, l=l: e.matmul(PS_S[0:1, 0:8], ONES32[0:1, 0:1],
                                          GB[0:1, l * 8:(l + 1) * 8],
                                          start=False, stop=True),
                  deps=[ones32_tok, gb_s.for_read(), last])
        gb_s.read(gl)
        bank.wrote(gl)
        lg = GS[0:1, 72:80]
        lgsl = stslot("lgrow", 0)
        t_cp = p.op("vector", lambda e: e.tensor_copy(lg, PS_S[0:1, 0:8]),
                    deps=[gl] + lgsl.for_write())
        bank.read(t_cp)
        lgsl.wrote(t_cp)
        trow = emit_topk(lg, [t_cp], GS[0:1, 48:56], stslot("cwrow", 0), 0, nrows=1)
        lgsl.read(trow)
        tfl = p.op("vector",
                   lambda e: e.tensor_single_scalar(FLAGS[0:1, :], GS[0:1, 48:56],
                                                    0.0, ALU.is_gt),
                   deps=[trow])
        seq_gate_tok[0] = tfl
        bb = p.op("tensor",
                  lambda e: e.matmul(PS_S[:, 8:16], ONES32[0:1, 0:128], GS[0:1, 48:56],
                                     start=True, stop=True),
                  deps=[trow, ones32_tok] + bank.for_write())
        stslot("cwrow", 0).read(bb)
        bank.wrote(bb)
        tc = p.op("vector", lambda e: e.tensor_copy(CWS[:, 0:8], PS_S[:, 8:16]),
                  deps=[bb] + cws_s.for_write())
        bank.read(tc)
        cws_s.wrote(tc)

    def emit_moe(l):
        for ex in range(NEXP):
            if l == 1:
                p.cond_ctx = (("m1", ex), FLAGS[0:1, ex:ex + 1], seq_gate_tok[0])
            tb = dma("sync", BRT[:, :],
                     d_eb1.ap()[(l * NEXP + ex) * 32:(l * NEXP + ex + 1) * 32, :],
                     deps=brt_s.for_write())
            brt_s.wrote(tb)
            bank = bM[0]
            tt = p.op("tensor",
                      lambda e: e.transpose(PSM[0][:, 0:32], BRT[:, :], EYE32[0:32, 0:32]),
                      deps=[tb, eye32_tok] + bank.for_write())
            bank.wrote(tt)
            brt_s.read(tt)
            tc = p.op("vector", lambda e: e.tensor_copy(B1C[:, :], PSM[0][:, 0:32]),
                      deps=[tt] + b1c_s.for_write())
            bank.read(tc)
            b1c_s.wrote(tc)
            load_row_bf16(d_eb2.ap()[l * NEXP + ex: l * NEXP + ex + 1, :], 0, brb[0])

            for ph in range(NPHASE):
                last_gelu = None
                w2cast_last = None
                for j in range(JCH):
                    J = ph * JCH + j
                    rowbase = (l * NEXP + ex) * DFF + J * 128
                    w2_ap, w2_tok = w2load(j, d_ew2.ap()[rowbase: rowbase + 128, :])
                    w2cast_last = w2_tok
                    colbase = (l * NEXP + ex) * D
                    s1_, w1_ap, w1_tok = wload(
                        colslice(d_ew1.ap()[colbase: colbase + D, :][:, J * 128:(J + 1) * 128]))
                    for grp in range(NGRP):
                        bank = bM[(j * NGRP + grp) % 2]
                        mmdeps = [w1_tok, hf.for_read()] + bank.for_write()
                        last = None
                        for c in range(NCH):
                            last = p.op("tensor",
                                        lambda e, c=c, grp=grp, w=w1_ap,
                                        bk=((j * NGRP + grp) % 2), st=(c == 0):
                                        e.matmul(PSM[bk][:, :],
                                                 w[:, c * 128:(c + 1) * 128],
                                                 HF[:, c * T + grp * 512: c * T + grp * 512 + 512],
                                                 start=st, stop=(c == NCH - 1)),
                                        deps=(mmdeps if c == 0 else [w1_tok]))
                        bank.wrote(last)
                        hf.read(last)
                        wuse(s1_, last)
                        gt = p.op("scalar",
                                  lambda e, j=j, grp=grp, bk=((j * NGRP + grp) % 2), J=J:
                                  e.activation(CB[:, j * T + grp * 512: j * T + grp * 512 + 512],
                                               PSM[bk][:, :], AF.Gelu,
                                               bias=B1C[:, J:J + 1]),
                                  deps=[last, tc] + cb.for_write())
                        bank.read(gt)
                        cb.wrote(gt)
                        last_gelu = gt
                b1c_s.read(last_gelu)
                last_mm2 = None
                for t in range(TT):
                    half = t % 2
                    off = half * 1024
                    banks = [bS[half * 2], bS[half * 2 + 1]]
                    mmdeps = ([cb.for_read(), last_gelu, w2cast_last]
                              + banks[0].for_write() + banks[1].for_write())
                    last = None
                    for j in range(JCH):
                        for piece in range(2):
                            fin = (ph > 0 and j == JCH - 1)
                            last = p.op("tensor",
                                        lambda e, j=j, piece=piece, t=t, off=off, fin=fin:
                                        e.matmul(PS_S[:, off + piece * 512: off + piece * 512 + 512],
                                                 CB[:, j * T + t * 128: j * T + t * 128 + 128],
                                                 WB2[:, j * 1024 + piece * 512:
                                                     j * 1024 + (piece + 1) * 512],
                                                 start=(j == 0), stop=fin),
                                        deps=(mmdeps if (j == 0 and piece == 0) else []))
                    if ph == 0:
                        for piece in range(2):
                            last = p.op("tensor",
                                        lambda e, piece=piece, off=off:
                                        e.matmul(PS_S[:, off + piece * 512: off + piece * 512 + 512],
                                                 ONESB[0:1, 0:128],
                                                 BRB[0:1, piece * 512:(piece + 1) * 512],
                                                 start=False, stop=True),
                                        deps=[onesb_tok, brb[0].for_read()])
                        brb[0].read(last)
                    banks[0].wrote(last)
                    banks[1].wrote(last)
                    cb.read(last)
                    last_mm2 = last
                    col = (CW[:, t * 8 + ex: t * 8 + ex + 1] if l == 0
                           else CWS[:, ex:ex + 1])
                    cwdep = (cw_s if l == 0 else cws_s).for_read()
                    t_add = p.op("vector",
                                 lambda e, t=t, off=off, col=col: e.scalar_tensor_tensor(
                                     Asl(t), PS_S[:, off:off + 1024], col, Asl(t),
                                     ALU.mult, ALU.add),
                                 deps=[last, a_w[t]] + cwdep + a_r[t])
                    banks[0].read(t_add)
                    banks[1].read(t_add)
                    a_w[t] = t_add
                    a_r[t] = []
                for j in range(JCH):
                    wb2_users[j].append(last_mm2)
            p.cond_ctx = None

    def emit_final(seq):
        for t in range(TT):
            par = t % 2
            m, rstd, t_r = ln_stats(t, Asl(t), [a_w[t]], par)
            ntap = NT32[:, par * 1024:(par + 1) * 1024]
            t_ap = p.op("vector",
                        lambda e, t=t, ntap=ntap, m=m, rstd=rstd: e.tensor_scalar(
                            ntap, Asl(t), m, rstd, ALU.subtract, ALU.mult),
                        deps=[a_w[t], t_r] + nt32[par].for_write())
            a_r[t].append(t_ap)
            nt32[par].wrote(t_ap)
            stslot("ln", par).read(t_ap)
            t_ld = dma("gpsimd", Asl(t), d_xsave.ap()[t * 128:(t + 1) * 128, :],
                       deps=[t_ap, a_w[t]] + a_r[t])
            t_add = p.op("vector",
                         lambda e, t=t, ntap=ntap: e.tensor_tensor(
                             Asl(t), Asl(t), ntap, ALU.add),
                         deps=[t_ap, t_ld])
            nt32[par].read(t_add)
            a_w[t] = t_add
            a_r[t] = []
            t_st = dma("gpsimd", d_out.ap()[seq * T + t * 128: seq * T + (t + 1) * 128, :],
                       Asl(t), deps=[t_add])
            a_r[t].append(t_st)
            out_toks.append(t_st)

    def emit_debug_out(seq):
        for t in range(TT):
            t_st = dma("gpsimd", d_out.ap()[seq * T + t * 128: seq * T + (t + 1) * 128, :],
                       Asl(t), deps=[a_w[t]] + a_r[t])
            a_r[t].append(t_st)
            out_toks.append(t_st)

    stopped = False
    for seq in range(n_seq):
        if stopped:
            break
        xdeps = []
        for t in range(TT):
            xdeps.append(a_w[t])
            xdeps.extend(a_r[t])
        xt = dma("sync", A[:, :].rearrange("p (t d) -> p t d", d=1024),
                 d_x.ap()[seq * T:(seq + 1) * T, :].rearrange("(t p) d -> p t d", p=128),
                 deps=xdeps)
        for t in range(TT):
            a_w[t] = xt
            a_r[t] = []
        if debug_stop == "x0":
            emit_debug_out(seq)
            continue

        load_row_bf16(d_bout.ap(), 1024, brb[1])
        for t in range(TT):
            emit_ln_to_hf(t, with_lnc=False)
        emit_attention(0, outer=True)
        if debug_stop == "x1":
            emit_debug_out(seq)
            continue

        for t in range(TT):
            t_sp = dma("gpsimd", d_xsave.ap()[t * 128:(t + 1) * 128, :], Asl(t),
                       deps=[a_w[t]] + a_r[t])
            a_r[t].append(t_sp)
            par = t % 2
            m, rstd, t_r = ln_stats(t, Asl(t), [a_w[t]], par)
            t_ap = p.op("vector",
                        lambda e, t=t, m=m, rstd=rstd: e.tensor_scalar(
                            Asl(t), Asl(t), m, rstd, ALU.subtract, ALU.mult),
                        deps=[t_r, t_sp, a_w[t]] + a_r[t])
            stslot("ln", par).read(t_ap)
            a_w[t] = t_ap
            a_r[t] = []
        if debug_stop == "y0":
            emit_debug_out(seq)
            continue

        for l in range(NL):
            tb = dma("sync", BRT[:, :], d_lnstack.ap()[l * 32:(l + 1) * 32, :],
                     deps=brt_s.for_write())
            brt_s.wrote(tb)
            bank = bM[1]
            tt1_ = p.op("tensor",
                        lambda e: e.transpose(PSM[1][:, 0:32], BRT[:, :], EYE32[0:32, 0:32]),
                        deps=[tb, eye32_tok] + bank.for_write())
            bank.wrote(tt1_)
            brt_s.read(tt1_)
            tc1 = p.op("vector", lambda e: e.tensor_copy(LNC[:, :], PSM[1][:, 0:32]),
                       deps=[tt1_] + lnc_s.for_write())
            bank.read(tc1)
            lnc_s.wrote(tc1)
            tb2 = dma("sync", BRT[0:16, :], d_bqk.ap()[l * 16:(l + 1) * 16, :],
                      deps=brt_s.for_write())
            brt_s.wrote(tb2)
            tt2_ = p.op("tensor",
                        lambda e: e.transpose(PSM[1][:, 64:80], BRT[0:16, :],
                                              EYE32[0:16, 0:16]),
                        deps=[tb2, eye32_tok] + bank.for_write())
            bank.wrote(tt2_)
            brt_s.read(tt2_)
            tc2 = p.op("vector", lambda e: e.tensor_copy(BQC[:, :], PSM[1][:, 64:80]),
                       deps=[tt2_] + bqc_s.for_write())
            bank.read(tc2)
            tc2b = p.op("vector",
                        lambda e: e.tensor_scalar(BQC[:, 0:8], BQC[:, 0:8], SCALE, None,
                                                  ALU.mult),
                        deps=[tc2])
            bqc_s.wrote(tc2b)
            load_row_bf16(d_bv.ap()[l:l + 1, :], 2048, brb[3])
            load_row_bf16(d_bo.ap()[l:l + 1, :], 1024, brb[1])

            for t in range(TT):
                emit_ln_to_hf(t, with_lnc=True)
            emit_attention(l, outer=False)
            if debug_stop == ("l0attn" if l == 0 else "l1attn"):
                emit_debug_out(seq)
                stopped = True
                break
            emit_ln2_gate(l)
            if l == 1:
                emit_seq_gate(l)
            emit_moe(l)
            if debug_stop == "l0" and l == 0:
                emit_debug_out(seq)
                stopped = True
                break
        if stopped:
            break
        if debug_stop == "y4":
            emit_debug_out(seq)
            continue
        emit_final(seq)

    p.op("gpsimd", lambda e: e.nop(), deps=out_toks)

    p.finalize_and_emit(es.enter_context(nc.Block()), sems)
    es.close()
    return nc, p


def _host_inputs(inputs, n_cores=8, t_tiles=16, n_seq=2):
    f32 = np.float32
    g = lambda k: np.ascontiguousarray(np.asarray(inputs[k], dtype=f32))
    T = t_tiles * 128
    x = g("L0_attn_x")
    base = {
        "wqkv": g("L0_attn_w_qkv"),
        "wout": g("L0_attn_w_out"),
        "bout": g("L0_attn_b_out").reshape(1, D),
        "wq": g("L0_ffn_wq").reshape(NL * D, D),
        "wk": g("L0_ffn_wk").reshape(NL * D, D),
        "wv": g("L0_ffn_wv").reshape(NL * D, D),
        "wo": g("L0_ffn_wo").reshape(NL * D, D),
        "lnstack": np.ascontiguousarray(
            np.stack([g("L0_ffn_ln1_w"), g("L0_ffn_ln1_b"),
                      g("L0_ffn_ln2_w"), g("L0_ffn_ln2_b")],
                     axis=1).reshape(NL * 32, 128)),
        "bqk": np.ascontiguousarray(
            np.stack([g("L0_ffn_bq"), g("L0_ffn_bk")], axis=1).reshape(NL * 16, 128)),
        "bv": g("L0_ffn_bv"),
        "bo": g("L0_ffn_bo"),
        "gatew": g("L0_ffn_gate_w").reshape(NL * D, NEXP),
        "gateb": g("L0_ffn_gate_b").reshape(1, NL * NEXP),
        "ew1": g("L0_ffn_e_w1").reshape(NL * NEXP * D, DFF),
        "eb1": g("L0_ffn_e_b1").reshape(NL * NEXP * 32, 128),
        "ew2": g("L0_ffn_e_w2").reshape(NL * NEXP * DFF, D),
        "eb2": g("L0_ffn_e_b2").reshape(NL * NEXP, D),
        "eyeb": np.eye(128, dtype=ml_dtypes.bfloat16),
        "eye32": np.eye(128, dtype=f32),
        "onesb": np.ones((1, 128), dtype=ml_dtypes.bfloat16),
        "ones32": np.ones((1, 128), dtype=f32),
        "mask": np.where(np.tril(np.ones((128, 128), dtype=bool)), 0.0,
                         BIGNEG).astype(f32),
    }
    in_maps = []
    for c in range(n_cores):
        m = dict(base)
        xs_ = x[c * n_seq:(c + 1) * n_seq, :T, :]
        m["x_in"] = np.ascontiguousarray(xs_.reshape(n_seq * T, D))
        in_maps.append(m)
    return in_maps


def kernel(**inputs):
    n_cores = 8
    t_tiles = 16
    n_seq = 2
    nc, _ = build_program(t_tiles=t_tiles, n_seq=n_seq)
    in_maps = _host_inputs(inputs, n_cores, t_tiles, n_seq)
    res = run_bass_kernel_spmd(nc, in_maps, core_ids=list(range(n_cores)))
    T = t_tiles * 128
    out = np.concatenate(
        [np.asarray(r["out"]).reshape(n_seq, T, D) for r in res.results], axis=0)
    return out
